# version 03
# baseline (speedup 1.0000x reference)
import numpy as np
import ml_dtypes
from contextlib import ExitStack

import concourse.bass as bass
import concourse.mybir as mybir
from concourse.bass_utils import run_bass_kernel_spmd

F32 = mybir.dt.float32
BF16 = mybir.dt.bfloat16
I16 = mybir.dt.int16
U8 = mybir.dt.uint8
AF = mybir.ActivationFunctionType
OP = mybir.AluOpType
AX = mybir.AxisListType
ds = bass.ds

D = 1024
T = 2048
NTG = 4
TG = 512
NFC = D // 128
I_FFN = 2816
NH = 16
HD = 64
OFFSETS = np.array(sorted(set(range(0, 33)) | {48, 64, 96, 128, 192, 256, 384, 512, 768, 1024, 1536}), dtype=np.int32)
MAXOFF = 1536
ND = 13
WIN = MAXOFF + T
NWC = WIN // 128
PAIRS = [[0, 1], [2, 3], [4, 5], [6, 7]]


class Env:

    GLOBAL_SEMS = ("_vch", "_sch", "consts", "cid", "cc")

    def __init__(self, nc):
        self.nc = nc
        self.ctx = ExitStack()
        self._stack = [self.ctx]
        self._sems = {}
        self._alias = {}
        self.prefix = ""

    def push(self, prefix):
        self._stack.append(ExitStack())
        self.prefix = prefix
        self._alias = {}

    def pop(self):
        self._stack.pop().close()
        self.prefix = ""
        self._alias = {}

    def sbuf(self, name, shape, dt=F32):
        return self._stack[-1].enter_context(self.nc.sbuf_tensor(self.prefix + name, shape, dt))

    def psum(self, name, shape, dt=F32):
        return self._stack[-1].enter_context(self.nc.psum_tensor(self.prefix + name, shape, dt))

    def sem(self, name):
        if name in self.GLOBAL_SEMS:
            phys = name
        else:
            if name not in self._alias:
                self._alias[name] = f"s{len(self._alias)}"
            phys = self._alias[name]
        if phys not in self._sems:
            h = self.ctx.enter_context(self.nc.semaphore(phys))
            self._sems[phys] = [h, 0]
        return self._sems[phys]

    def inc(self, name, inst, k=1):
        s = self.sem(name)
        inst.then_inc(s[0], k)
        s[1] += k
        return s[1]

    def wait(self, name, eng):
        s = self.sem(name)
        eng.wait_ge(s[0], s[1])

    def wait_val(self, name, eng, val):
        eng.wait_ge(self.sem(name)[0], val)

    def val(self, name):
        return self.sem(name)[1]

    def vsync(self, inst):
        n = self.inc("_vch", inst)
        self.nc.vector.wait_ge(self.sem("_vch")[0], n)
        return inst

    def ssync(self, inst):
        n = self.inc("_sch", inst)
        self.nc.scalar.wait_ge(self.sem("_sch")[0], n)
        return inst

    def barrier(self):
        self.nc.all_engine_barrier()

    def close(self):
        while len(self._stack) > 1:
            self.pop()
        self.ctx.close()


def common_consts(env):
    nc = env.nc
    c = {}
    c["ones_col_f"] = env.sbuf("c_ones_col_f", [128, 1], F32)
    c["ones_col_b"] = env.sbuf("c_ones_col_b", [128, 1], BF16)
    c["ones2_f"] = env.sbuf("c_ones2_f", [2, 128], F32)
    c["ones_row_b"] = env.sbuf("c_ones_row_b", [1, 128], BF16)
    c["neghalf_row_b"] = env.sbuf("c_neghalf_row_b", [1, 128], BF16)
    v = nc.vector
    v.memset(c["ones_col_f"][:, :], 1.0)
    v.memset(c["ones_col_b"][:, :], 1.0)
    v.memset(c["ones2_f"][:, :], 0.0)
    v.memset(c["ones_row_b"][:, :], 1.0)
    v.memset(c["neghalf_row_b"][:, :], -0.5)
    last = v.memset(c["ones2_f"][0:1, :], 1.0)
    for i, val in enumerate([1e-5, -0.5, 0.125]):
        t = env.sbuf(f"c_cap{i}", [1, 1], F32)
        last = v.memset(t[:, :], val)
        nc.const_aps.register(F32, val, t[0:1, 0:1])
    env.inc("consts", last)
    for e in (nc.tensor, nc.scalar, nc.sync, nc.gpsimd):
        env.wait("consts", e)
    return c


def load_idents(env, c, idb, idf):
    nc = env.nc
    c["ident"] = env.sbuf("c_ident", [128, 128], BF16)
    c["identf"] = env.sbuf("c_identf", [128, 128], F32)
    env.inc("cid", nc.gpsimd.dma_start(out=c["ident"][:, :], in_=idb), 16)
    env.inc("cid", nc.gpsimd.dma_start(out=c["identf"][:, :], in_=idf), 16)
    for e in (nc.tensor, nc.vector, nc.scalar, nc.sync, nc.gpsimd):
        env.wait("cid", e)



def fp4_quant_scratch(env, Fmax, tag=""):
    return {
        "amax": env.sbuf(f"q_amax{tag}", [128, Fmax // 16], F32),
        "sc": env.sbuf(f"q_sc{tag}", [128, Fmax // 16], F32),
        "rs": env.sbuf(f"q_rs{tag}", [128, Fmax // 16], F32),
        "xn": env.sbuf(f"q_xn{tag}", [128, Fmax], F32),
        "i16a": env.sbuf(f"q_i16a{tag}", [128, Fmax], I16),
        "i16b": env.sbuf(f"q_i16b{tag}", [128, Fmax], I16),
        "t1": env.sbuf(f"q_t1{tag}", [128, Fmax], F32),
        "A": env.sbuf(f"q_A{tag}", [128, Fmax], F32),
        "B": env.sbuf(f"q_B{tag}", [128, Fmax], F32),
        "q1": env.sbuf(f"q_q1{tag}", [128, Fmax], F32),
    }


def emit_fp4_quant(env, x_in, out_bf, scr):
    v = env.nc.vector
    P, F = x_in.shape
    NB = F // 16
    amax, sc, rs = scr["amax"][:P, :NB], scr["sc"][:P, :NB], scr["rs"][:P, :NB]
    xn = scr["xn"][:P, :F]
    i16a, i16b = scr["i16a"][:P, :F], scr["i16b"][:P, :F]
    t1, A, B, q1 = scr["t1"][:P, :F], scr["A"][:P, :F], scr["B"][:P, :F], scr["q1"][:P, :F]
    x3 = x_in.rearrange("p (b k) -> p b k", k=16)
    env.vsync(v.tensor_reduce(amax, x3, AX.X, OP.max, apply_absolute_value=True))
    env.vsync(v.tensor_scalar(sc, amax, 1.0 / 6.0, 1e-12, OP.mult, OP.max))
    env.vsync(v.reciprocal(rs, sc))
    rs_b = rs.unsqueeze(2).broadcast_to([P, NB, 16])
    sc_b = sc.unsqueeze(2).broadcast_to([P, NB, 16])
    xn3 = xn.rearrange("p (b k) -> p b k", k=16)
    env.vsync(v.tensor_tensor(xn3, x3, rs_b, OP.mult))
    v.tensor_scalar(i16a, xn, 2.0, None, OP.mult)
    env.vsync(v.tensor_scalar(i16b, xn, 1.0, None, OP.mult))
    v.tensor_scalar(t1, i16a, 0.5, 2.0, OP.mult, OP.min)
    v.tensor_scalar(A, i16b, 4.0, 2.0, OP.min, OP.max)
    env.vsync(v.tensor_scalar(B, i16b, -4.0, -2.0, OP.max, OP.min))
    env.vsync(v.scalar_tensor_tensor(q1, t1, -2.0, A, OP.max, OP.add))
    env.vsync(v.tensor_tensor(q1, q1, B, OP.add))
    v.tensor_scalar(A, xn, 5.0, 2.0, OP.is_gt, OP.mult)
    env.vsync(v.tensor_scalar(B, xn, -5.0, -2.0, OP.is_lt, OP.mult))
    env.vsync(v.tensor_tensor(q1, q1, A, OP.add))
    env.vsync(v.tensor_tensor(q1, q1, B, OP.add))
    o3 = out_bf.rearrange("p (b k) -> p b k", k=16)
    q13 = q1.rearrange("p (b k) -> p b k", k=16)
    return v.tensor_tensor(o3, q13, sc_b, OP.mult)



def ln_scratch(env, p_b1, p_b2, tag=""):
    return {
        "xsq": [env.sbuf(f"ln_xsq0{tag}", [128, TG], F32), env.sbuf(f"ln_xsq1{tag}", [128, TG], F32)],
        "p_s1": env.psum(f"ln_ps1{tag}", [1, TG], F32),
        "p_s2": env.psum(f"ln_ps2{tag}", [1, TG], F32),
        "p_b1": p_b1,
        "p_b2": p_b2,
        "m": env.sbuf(f"ln_m{tag}", [2, TG], F32),
        "msq": env.sbuf(f"ln_msq{tag}", [2, TG], F32),
        "var": env.sbuf(f"ln_var{tag}", [2, TG], F32),
        "lnv": env.sbuf(f"ln_lnv{tag}", [2, TG], F32),
        "rstd": env.sbuf(f"ln_rstd{tag}", [2, TG], F32),
        "mr": env.sbuf(f"ln_mr{tag}", [2, TG], F32),
        "tscr": env.sbuf(f"ln_tscr{tag}", [128, TG], F32),
    }


def emit_ln_fm(env, c, x_chunks, tg, xn_chunks, s):
    nc = env.nc
    v, te, se = nc.vector, nc.tensor, nc.scalar
    sl = slice(tg * TG, (tg + 1) * TG)
    mm0 = env.val("ln_mm")
    sq0 = env.val("ln_sq")
    for fc in range(NFC):
        inst = te.matmul(s["p_s1"][:, :], c["ones_col_f"][:, :], x_chunks[fc][:, sl],
                         start=(fc == 0), stop=(fc == NFC - 1))
    env.inc("ln_mm", inst)
    for fc in range(NFC):
        b = fc % 2
        if fc >= 2:
            env.wait_val("ln_mm", se, mm0 + fc)
        env.inc("ln_sq", se.activation(s["xsq"][b][:, :], x_chunks[fc][:, sl], AF.Square))
        env.wait_val("ln_sq", te, sq0 + fc + 1)
        inst = te.matmul(s["p_s2"][:, :], c["ones_col_f"][:, :], s["xsq"][b][:, :],
                         start=(fc == 0), stop=(fc == NFC - 1))
        env.inc("ln_mm", inst)
    env.wait("ln_mm", v)
    env.vsync(v.tensor_scalar(s["m"][0:1, :], s["p_s1"][:, :], 1.0 / D, None, OP.mult))
    env.vsync(v.tensor_tensor(s["msq"][0:1, :], s["m"][0:1, :], s["m"][0:1, :], OP.mult))
    env.inc("ln_dve", v.scalar_tensor_tensor(s["var"][0:1, :], s["p_s2"][:, :], 1.0 / D,
                                             s["msq"][0:1, :], OP.mult, OP.subtract))
    env.wait("ln_dve", se)
    env.ssync(se.activation(s["lnv"][0:1, :], s["var"][0:1, :], AF.Ln, bias=1e-5))
    env.inc("ln_sq", se.activation(s["rstd"][0:1, :], s["lnv"][0:1, :], AF.Exp, scale=-0.5))
    env.wait("ln_sq", v)
    env.inc("ln_dve", v.tensor_tensor(s["mr"][0:1, :], s["m"][0:1, :], s["rstd"][0:1, :], OP.mult))
    env.wait("ln_dve", te)
    te.matmul(s["p_b1"][:, :], c["ones2_f"][:, :], s["rstd"][:, :], start=True, stop=True)
    env.inc("ln_mm", te.matmul(s["p_b2"][:, :], c["ones2_f"][:, :], s["mr"][:, :], start=True, stop=True))
    env.wait("ln_mm", v)
    for fc in range(NFC):
        env.vsync(v.tensor_tensor(s["tscr"][:, :], x_chunks[fc][:, sl], s["p_b1"][:, :], OP.mult))
        env.vsync(v.tensor_tensor(xn_chunks[fc][:, 0:TG], s["tscr"][:, :], s["p_b2"][:, :], OP.subtract))
    env.inc("ln_done", v.engine_nop())



PREP_WEIGHTS = [
    ("L0_ffn_w_gate", "wgq_t", I_FFN, D, True),
    ("L0_ffn_w_up", "wuq_t", I_FFN, D, True),
    ("L0_ffn_w_down", "wdq_t", D, I_FFN, True),
    ("L0_attn_Wqkv", "qkv_t", 3 * D, D, False),
    ("L0_attn_Wgate", "wgate_t", D, D, False),
    ("L0_attn_Wout", "wout_t", D, D, False),
    ("wq_p", "wq_t", D, D, False),
    ("wk_p", "wk_t", D, D, False),
    ("L1_attn_wv", "wv_t", D, D, False),
    ("L1_attn_wo", "wo_t", D, D, False),
]


def emit_prep(env, c, ins, outs):
    nc = env.nc
    wbuf = env.sbuf("p_wbuf", [128, 24 * D], BF16)
    tstrip = env.sbuf("p_tstrip", [128, 3 * D], BF16)
    qscr = fp4_quant_scratch(env, I_FFN)
    qf32 = env.sbuf("p_qf32", [128, I_FFN], F32)

    sy, sc_, gp, v = nc.sync, nc.scalar, nc.gpsimd, nc.vector
    for nm, cname, O, F, quant in PREP_WEIGHTS:
        NOC, NFCw = O // 128, F // 128
        if quant:
            for oc in range(NOC):
                env.wait("pqd", gp)
                env.inc("pld", gp.dma_start(out=qf32[:, :F], in_=ins[nm][oc * 128:(oc + 1) * 128, :]), 16)
                env.wait("pld", v)
                env.inc("pqd", emit_fp4_quant(env, qf32[:, :F], wbuf[:, oc * F:(oc + 1) * F], qscr))
        else:
            for oc in range(NOC):
                env.inc("pld", gp.dma_start(out=wbuf[:, oc * F:(oc + 1) * F], in_=ins[nm][oc * 128:(oc + 1) * 128, :]), 16)
        env.wait("pqd", sy)
        env.wait("pld", sy)
        for fc in range(NFCw):
            for oc in range(NOC):
                env.inc("ptr", sy.dma_start(out=tstrip[:, oc * 128:(oc + 1) * 128],
                                            in_=wbuf[:, oc * F + fc * 128: oc * F + (fc + 1) * 128],
                                            transpose=True), 16)
            env.wait("ptr", sc_)
            env.inc("pst", sc_.dma_start(out=outs[cname][fc, :, :, :].transpose([1, 0, 2]),
                                         in_=tstrip[:, 0:O].rearrange("p (oc k) -> p oc k", k=128)), 16)
            env.wait("pst", sy)
        env.barrier()

    k0 = ins["L1_ffn_k0"]
    top = env.sbuf("p_k0t", [128, 4096], BF16)
    bot = env.sbuf("p_k0b", [128, 4096], BF16)
    dif = env.sbuf("p_k0d", [128, 4096], BF16)
    for fc in range(NFC):
        env.inc("pld", gp.dma_start(out=top[:, :], in_=k0[fc * 128:(fc + 1) * 128, :]), 16)
        env.inc("pld", gp.dma_start(out=bot[:, :], in_=k0[D + fc * 128:D + (fc + 1) * 128, :]), 16)
        env.wait("pld", v)
        env.inc("pqd", v.tensor_tensor(dif[:, :], top[:, :], bot[:, :], OP.subtract))
        env.wait("pqd", sy)
        env.inc("pst2", sy.dma_start(out=outs["beff_t"][:, fc, :, :].transpose([1, 0, 2]),
                                     in_=dif[:, :].rearrange("p (jc k) -> p jc k", k=128)), 16)
        env.barrier()
    env.wait("pst2", sy)
    env.wait("pst", sy)


def build_prep():
    nc = bass.Bass("TRN2", target_bir_lowering=False)
    env = Env(nc)
    ins = {}
    outs = {}
    for nm, cname, O, F, _ in PREP_WEIGHTS:
        ins[nm] = nc.dram_tensor(nm, [O, F], F32, kind="ExternalInput").ap()
        outs[cname] = nc.dram_tensor(cname, [F // 128, O // 128, 128, 128], BF16, kind="ExternalOutput").ap()
    ins["L1_ffn_k0"] = nc.dram_tensor("L1_ffn_k0", [2 * D, 4096], U8, kind="ExternalInput").ap()
    outs["beff_t"] = nc.dram_tensor("beff_t", [32, 8, 128, 128], BF16, kind="ExternalOutput").ap()
    c = common_consts(env)
    emit_prep(env, c, ins, outs)
    env.close()
    return nc



def emit_l0ffn(env, c, io):
    nc = env.nc
    x_io, x_out = io["x_in"], io["x_out"]
    wgq, wuq, wdq = io["wgq"], io["wuq"], io["wdq"]
    ident, identf = c["ident"], c["identf"]

    x_sb = [env.sbuf(f"x{fc}", [128, T], F32) for fc in range(NFC)]
    xn_sb = [env.sbuf(f"xn{fc}", [128, TG], F32) for fc in range(NFC)]
    for fc in range(NFC):
        env.inc("cld", nc.gpsimd.dma_start(out=x_sb[fc][:, :], in_=x_io[fc * 128:(fc + 1) * 128, :]), 16)
    for e in (nc.vector, nc.scalar, nc.tensor, nc.sync):
        env.wait("cld", e)

    PA = env.psum("PA", [128, TG], F32)
    PB = env.psum("PB", [128, TG], F32)
    PD = env.psum("PD", [128, TG], F32)
    PTF = env.psum("PTF", [128, 128], F32)
    PTB = env.psum("PTB", [128, 128], BF16)
    lns = ln_scratch(env, PA, PB)
    qscr = fp4_quant_scratch(env, D)

    xn_tm = env.sbuf("xn_tm", [128, 4 * D], F32)
    xq_tm = env.sbuf("xq_tm", [128, 4 * D], BF16)
    xq_fm = [env.sbuf(f"xqf{fc}", [128, TG], BF16) for fc in range(NFC)]
    w_rhs = env.sbuf("w_rhs", [128, 2 * NFC * TG], BF16)
    hs = env.sbuf("hs", [128, TG], F32)
    h_tm = env.sbuf("h_tm", [128, TG], F32)
    hq_tm = env.sbuf("hq_tm", [128, TG], BF16)
    hq_fm = [env.sbuf(f"hqf{j}", [128, TG], BF16) for j in range(22)]
    wd_sb = env.sbuf("wd_sb", [128, 22 * 128], BF16)

    te, v, se, sy, gp = nc.tensor, nc.vector, nc.scalar, nc.sync, nc.gpsimd

    for tg in range(NTG):
        sl = slice(tg * TG, (tg + 1) * TG)
        emit_ln_fm(env, c, x_sb, tg, xn_sb, lns)
        env.barrier()
        for tc in range(4):
            for fc in range(NFC):
                env.wait("tq", te)
                env.inc("ttr", te.transpose(PTF[:, :], xn_sb[fc][:, tc * 128:(tc + 1) * 128], identf[:, :]))
                env.wait("ttr", v)
                env.inc("tq", v.tensor_copy(xn_tm[:, tc * D + fc * 128: tc * D + (fc + 1) * 128], PTF[:, :]))
        env.barrier()
        for tc in range(4):
            env.inc("qx", emit_fp4_quant(env, xn_tm[:, tc * D:(tc + 1) * D], xq_tm[:, tc * D:(tc + 1) * D], qscr))
        env.wait("qx", te)
        for tc in range(4):
            for fc in range(NFC):
                env.wait("tq2", te)
                env.inc("ttr2", te.transpose(PTB[:, :], xq_tm[:, tc * D + fc * 128: tc * D + (fc + 1) * 128], ident[:, :]))
                env.wait("ttr2", v)
                env.inc("tq2", v.tensor_copy(xq_fm[fc][:, tc * 128:(tc + 1) * 128], PTB[:, :]))
        env.barrier()
        for og in range(6):
            OW = TG if og < 5 else 256
            noc = OW // 128
            for fc in range(NFC):
                env.inc("wld", gp.dma_start(
                    out=w_rhs[:, fc * TG: fc * TG + OW].rearrange("p (oc k) -> p oc k", k=128),
                    in_=wgq[fc, og * 4:og * 4 + noc, :, :].transpose([1, 0, 2])), 16)
                env.inc("wld", gp.dma_start(
                    out=w_rhs[:, (NFC + fc) * TG: (NFC + fc) * TG + OW].rearrange("p (oc k) -> p oc k", k=128),
                    in_=wuq[fc, og * 4:og * 4 + noc, :, :].transpose([1, 0, 2])), 16)
            env.wait("wld", te)
            for tc in range(4):
                for fc in range(NFC):
                    te.matmul(PA[:, :OW], xq_fm[fc][:, tc * 128:(tc + 1) * 128],
                              w_rhs[:, fc * TG: fc * TG + OW], start=(fc == 0), stop=(fc == NFC - 1))
                for fc in range(NFC):
                    inst = te.matmul(PB[:, :OW], xq_fm[fc][:, tc * 128:(tc + 1) * 128],
                                     w_rhs[:, (NFC + fc) * TG: (NFC + fc) * TG + OW],
                                     start=(fc == 0), stop=(fc == NFC - 1))
                env.inc("gumm", inst)
                env.wait("gumm", se)
                env.inc("silu", se.activation(hs[:, :OW], PA[:, :OW], AF.Silu))
                env.wait("silu", v)
                env.vsync(v.tensor_tensor(h_tm[:, :OW], hs[:, :OW], PB[:, :OW], OP.mult))
                env.inc("hq", emit_fp4_quant(env, h_tm[:, :OW], hq_tm[:, :OW], qscr))
                env.wait("hq", te)
                for j in range(noc):
                    env.inc("ttr3", te.transpose(PTB[:, :], hq_tm[:, j * 128:(j + 1) * 128], ident[:, :]))
                    env.wait("ttr3", v)
                    env.inc("tq3", v.tensor_copy(hq_fm[og * 4 + j][:, tc * 128:(tc + 1) * 128], PTB[:, :]))
                    env.wait("tq3", te)
            env.barrier()
        for dc in range(NFC):
            env.wait("dmm", gp)
            env.inc("wld2", gp.dma_start(out=wd_sb[:, :].rearrange("p (j k) -> p j k", k=128),
                                         in_=wdq[:, dc, :, :].transpose([1, 0, 2])), 16)
            env.wait("wld2", te)
            for j in range(22):
                inst = te.matmul(PD[:, :], wd_sb[:, j * 128:(j + 1) * 128], hq_fm[j][:, :],
                                 start=(j == 0), stop=(j == 21))
            env.inc("dmm", inst)
            env.wait("dmm", v)
            env.inc("resid", v.scalar_tensor_tensor(x_sb[dc][:, sl], PD[:, :], 0.5, x_sb[dc][:, sl],
                                                    OP.mult, OP.add))
            env.wait("resid", te)
        env.barrier()

    for fc in range(NFC):
        env.inc("xst", sy.dma_start(out=x_out[fc * 128:(fc + 1) * 128, :], in_=x_sb[fc][:, :]), 16)
    env.wait("xst", sy)


def build_l0ffn():
    nc = bass.Bass("TRN2", target_bir_lowering=False)
    env = Env(nc)
    io = dict(
        x_in=nc.dram_tensor("x_fm", [D, T], F32, kind="ExternalInput").ap(),
        x_out=nc.dram_tensor("x_fm_out", [D, T], F32, kind="ExternalOutput").ap(),
        wgq=nc.dram_tensor("wgq_t", [NFC, 22, 128, 128], BF16, kind="ExternalInput").ap(),
        wuq=nc.dram_tensor("wuq_t", [NFC, 22, 128, 128], BF16, kind="ExternalInput").ap(),
        wdq=nc.dram_tensor("wdq_t", [22, NFC, 128, 128], BF16, kind="ExternalInput").ap(),
    )
    idb = nc.dram_tensor("id_bf", [128, 128], BF16, kind="ExternalInput").ap()
    idf = nc.dram_tensor("id_f32", [128, 128], F32, kind="ExternalInput").ap()
    c = common_consts(env)
    load_idents(env, c, idb, idf)
    emit_l0ffn(env, c, io)
    env.close()
    return nc



def emit_l1ffn(env, c, io):
    nc = env.nc
    x_io, x_out = io["x_in"], io["x_out"]
    beff, k1t, k2t, s_in = io["beff"], io["k1t"], io["k2t"], io["s_in"]

    x_sb = [env.sbuf(f"x{fc}", [128, T], F32) for fc in range(NFC)]
    xn_sb = [env.sbuf(f"xn{fc}", [128, TG], BF16) for fc in range(NFC)]
    for fc in range(NFC):
        env.inc("cld", nc.gpsimd.dma_start(out=x_sb[fc][:, :], in_=x_io[fc * 128:(fc + 1) * 128, :]), 16)

    te, v, se, sy, gp = nc.tensor, nc.vector, nc.scalar, nc.sync, nc.gpsimd

    s_sb = env.sbuf("s_sb", [2, 3], F32)
    env.inc("cld", gp.dma_start(out=s_sb[0:1, :], in_=s_in.transpose([1, 0])), 16)
    for e in (nc.vector, nc.scalar, nc.tensor, nc.sync):
        env.wait("cld", e)
    PA = env.psum("PA", [128, TG], F32)
    PB = env.psum("PB", [128, TG], F32)
    PD = env.psum("PD", [128, TG], F32)
    PS = env.psum("PS", [1, TG], F32)
    mb = env.sbuf("mb", [128, 3], F32)
    env.inc("smm", te.matmul(PA[:, 0:3], c["ones2_f"][:, :], s_sb[:, :], start=True, stop=True))
    env.wait("smm", v)
    v.tensor_scalar(mb[:, 0:1], PA[:, 0:1], 2.0, None, OP.mult)
    v.tensor_scalar(mb[:, 1:2], PA[:, 1:2], 4.0, None, OP.mult)
    env.inc("sdv", v.tensor_scalar(mb[:, 2:3], PA[:, 2:3], 2.0, None, OP.mult))
    env.wait("sdv", te)
    env.barrier()

    lns = ln_scratch(env, PA, PB)
    a1 = [env.sbuf(f"a1_{j}", [128, TG], BF16) for j in range(32)]
    a2 = [env.sbuf(f"a2_{j}", [128, TG], BF16) for j in range(32)]
    w_sb = env.sbuf("w_sb", [128, 32 * 128], BF16)
    rs_hi = env.sbuf("rs_hi", [1, TG], BF16)
    rs_lo = env.sbuf("rs_lo", [1, TG], BF16)

    for tg in range(NTG):
        sl = slice(tg * TG, (tg + 1) * TG)
        emit_ln_fm(env, c, x_sb, tg, xn_sb, lns)
        env.barrier()
        for jc in range(32):
            env.wait("l1mm", gp)
            env.inc("l1ld", gp.dma_start(out=w_sb[:, 0:NFC * 128].rearrange("p (j k) -> p j k", k=128),
                                         in_=beff[jc, :, :, :].transpose([1, 0, 2])), 16)
            env.wait("l1ld", te)
            for fc in range(NFC):
                inst = te.matmul(PD[:, :], w_sb[:, fc * 128:(fc + 1) * 128], xn_sb[fc][:, 0:TG],
                                 start=(fc == 0), stop=(fc == NFC - 1))
            env.inc("l1mm", inst)
            env.wait("l1mm", v)
            env.inc("l1ev", v.tensor_scalar(a1[jc][:, :], PD[:, :], mb[:, 0:1], 0.0, OP.mult, OP.max))
            env.wait("l1ev", te)
        env.barrier()
        for jc in range(32):
            inst = te.matmul(PS[:, :], c["ones_col_b"][:, :], a1[jc][:, :],
                             start=(jc == 0), stop=(jc == 31))
        env.inc("l1mm", inst)
        env.wait("l1mm", v)
        env.vsync(v.tensor_copy(rs_hi[0:1, :], PS[:, :]))
        env.inc("l1ev", v.tensor_tensor(rs_lo[0:1, :], PS[:, :], rs_hi[0:1, :], OP.subtract))
        env.wait("l1ev", te)
        env.barrier()
        for jc in range(32):
            env.wait("l2mm", gp)
            env.inc("l2ld", gp.dma_start(out=w_sb[:, :].rearrange("p (j k) -> p j k", k=128),
                                         in_=k1t[jc, :, :, :].transpose([1, 0, 2])), 16)
            env.wait("l2ld", te)
            for fc in range(32):
                te.matmul(PD[:, :], w_sb[:, fc * 128:(fc + 1) * 128], a1[fc][:, :],
                          start=(fc == 0), stop=False)
            te.matmul(PD[:, :], c["neghalf_row_b"][0:1, :], rs_hi[0:1, :], start=False, stop=False)
            inst = te.matmul(PD[:, :], c["neghalf_row_b"][0:1, :], rs_lo[0:1, :],
                             start=False, stop=True)
            env.inc("l2mm", inst)
            env.wait("l2mm", v)
            env.inc("l2ev", v.tensor_scalar(a2[jc][:, :], PD[:, :], mb[:, 1:2], 0.0, OP.mult, OP.max))
            env.wait("l2ev", te)
        env.barrier()
        for jc in range(32):
            inst = te.matmul(PS[:, :], c["ones_col_b"][:, :], a2[jc][:, :],
                             start=(jc == 0), stop=(jc == 31))
        env.inc("l1mm", inst)
        env.wait("l1mm", v)
        env.vsync(v.tensor_copy(rs_hi[0:1, :], PS[:, :]))
        env.inc("l1ev", v.tensor_tensor(rs_lo[0:1, :], PS[:, :], rs_hi[0:1, :], OP.subtract))
        env.wait("l1ev", te)
        env.barrier()
        for dc in range(NFC):
            env.wait("l3mm", gp)
            env.inc("l3ld", gp.dma_start(out=w_sb[:, :].rearrange("p (j k) -> p j k", k=128),
                                         in_=k2t[dc, :, :, :].transpose([1, 0, 2])), 16)
            env.wait("l3ld", te)
            for fc in range(32):
                te.matmul(PD[:, :], w_sb[:, fc * 128:(fc + 1) * 128], a2[fc][:, :],
                          start=(fc == 0), stop=False)
            te.matmul(PD[:, :], c["neghalf_row_b"][0:1, :], rs_hi[0:1, :], start=False, stop=False)
            inst = te.matmul(PD[:, :], c["neghalf_row_b"][0:1, :], rs_lo[0:1, :],
                             start=False, stop=True)
            env.inc("l3mm", inst)
            env.wait("l3mm", v)
            env.vsync(v.tensor_scalar(lns["tscr"][:, :], PD[:, :], mb[:, 2:3], None, OP.mult))
            env.inc("l3ev", v.tensor_tensor(x_sb[dc][:, sl], x_sb[dc][:, sl], lns["tscr"][:, :], OP.add))
            env.wait("l3ev", te)
        env.barrier()

    for fc in range(NFC):
        env.inc("xst", sy.dma_start(out=x_out[fc * 128:(fc + 1) * 128, :], in_=x_sb[fc][:, :]), 16)
    env.wait("xst", sy)


def build_l1ffn():
    nc = bass.Bass("TRN2", target_bir_lowering=False)
    env = Env(nc)
    io = dict(
        x_in=nc.dram_tensor("x_fm", [D, T], F32, kind="ExternalInput").ap(),
        x_out=nc.dram_tensor("x_fm_out", [D, T], F32, kind="ExternalOutput").ap(),
        beff=nc.dram_tensor("beff_t", [32, NFC, 128, 128], BF16, kind="ExternalInput").ap(),
        k1t=nc.dram_tensor("k1_t", [32, 32, 128, 128], U8, kind="ExternalInput").ap(),
        k2t=nc.dram_tensor("k2_t", [NFC, 32, 128, 128], U8, kind="ExternalInput").ap(),
        s_in=nc.dram_tensor("s_sc", [3, 1], F32, kind="ExternalInput").ap(),
    )
    c = common_consts(env)
    emit_l1ffn(env, c, io)
    env.close()
    return nc



def emit_l1attn_a(env, c, io):
    nc = env.nc
    x_io = io["x_in"]
    wqt, wkt, wvt = io["wqt"], io["wkt"], io["wvt"]
    cos_q, sin_q, cos_k, sin_k = io["cos_q"], io["sin_q"], io["cos_k"], io["sin_k"]
    q_tm, k_tm, v_tm = io["q_tm"], io["k_tm"], io["v_tm"]

    te, v, se, sy, gp = nc.tensor, nc.vector, nc.scalar, nc.sync, nc.gpsimd

    x_sb = [env.sbuf(f"x{fc}", [128, T], F32) for fc in range(NFC)]
    xn_sb = [env.sbuf(f"xn{fc}", [128, TG], BF16) for fc in range(NFC)]
    for fc in range(NFC):
        env.inc("cld", gp.dma_start(out=x_sb[fc][:, :], in_=x_io[fc * 128:(fc + 1) * 128, :]), 16)
    wq_sb = env.sbuf("wq_sb", [128, NFC * D], BF16)
    wk_sb = env.sbuf("wk_sb", [128, NFC * D], BF16)
    wv_sb = env.sbuf("wv_sb", [128, NFC * D], BF16)
    for fc in range(NFC):
        env.inc("cld", gp.dma_start(out=wq_sb[:, fc * D:(fc + 1) * D].rearrange("p (oc k) -> p oc k", k=128),
                                    in_=wqt[fc, :, :, :].transpose([1, 0, 2])), 16)
        env.inc("cld", gp.dma_start(out=wk_sb[:, fc * D:(fc + 1) * D].rearrange("p (oc k) -> p oc k", k=128),
                                    in_=wkt[fc, :, :, :].transpose([1, 0, 2])), 16)
        env.inc("cld", gp.dma_start(out=wv_sb[:, fc * D:(fc + 1) * D].rearrange("p (oc k) -> p oc k", k=128),
                                    in_=wvt[fc, :, :, :].transpose([1, 0, 2])), 16)
    cq_sb = env.sbuf("cq_sb", [128, 2 * 16 * 32], F32)
    sq_sb = env.sbuf("sq_sb", [128, 2 * 16 * 32], F32)
    ck_sb = env.sbuf("ck_sb", [128, 2 * 16 * 32], F32)
    sk_sb = env.sbuf("sk_sb", [128, 2 * 16 * 32], F32)
    for dst, srcv in [(cq_sb, cos_q), (sq_sb, sin_q), (ck_sb, cos_k), (sk_sb, sin_k)]:
        env.inc("cld", gp.dma_start(out=dst[:, :].rearrange("p (par j i) -> p par j i", par=2, j=16),
                                    in_=srcv.rearrange("(par p j) i -> p par j i", par=2, p=128)), 16)
    for e in (nc.vector, nc.scalar, nc.tensor, nc.sync):
        env.wait("cld", e)

    PA = env.psum("PA", [128, TG], F32)
    PB = env.psum("PB", [128, TG], F32)
    PQ = env.psum("PQ", [128, TG], F32)
    lns = ln_scratch(env, PA, PB)
    sA = env.sbuf("sA", [128, 256], F32)
    sB = env.sbuf("sB", [128, 256], F32)
    sC = env.sbuf("sC", [128, 256], F32)
    sD = env.sbuf("sD", [128, 256], F32)
    qstage = env.sbuf("qstage", [128, D], BF16)
    kstage = env.sbuf("kstage", [128, D], BF16)
    vstage = env.sbuf("vstage", [128, NH * 65], BF16)

    def rope_evac(tci, half, Csb, Ssb, stage):
        ev = PQ[:, :].rearrange("p (h s) -> p h s", s=64)[:, :, 0:32]
        od = PQ[:, :].rearrange("p (h s) -> p h s", s=64)[:, :, 32:64]
        par = tci % 2
        Cb = Csb[:, :].rearrange("p (q j i) -> p q j i", q=2, j=16)[:, par, half * 8:(half + 1) * 8, :]
        Sb = Ssb[:, :].rearrange("p (q j i) -> p q j i", q=2, j=16)[:, par, half * 8:(half + 1) * 8, :]
        a3 = sA[:, :].rearrange("p (h s) -> p h s", s=32)
        b3 = sB[:, :].rearrange("p (h s) -> p h s", s=32)
        c3 = sC[:, :].rearrange("p (h s) -> p h s", s=32)
        d3 = sD[:, :].rearrange("p (h s) -> p h s", s=32)
        v.tensor_tensor(a3, ev, Cb, OP.mult)
        v.tensor_tensor(b3, od, Sb, OP.mult)
        v.tensor_tensor(c3, ev, Sb, OP.mult)
        env.vsync(v.tensor_tensor(d3, od, Cb, OP.mult))
        oev = stage[:, half * TG: (half + 1) * TG].rearrange("p (h s) -> p h s", s=64)[:, :, 0:32]
        ood = stage[:, half * TG: (half + 1) * TG].rearrange("p (h s) -> p h s", s=64)[:, :, 32:64]
        v.tensor_tensor(oev, a3, b3, OP.subtract)
        return v.tensor_tensor(ood, c3, d3, OP.add)

    for tg in range(NTG):
        emit_ln_fm(env, c, x_sb, tg, xn_sb, lns)
        env.barrier()
        for tc in range(4):
            tci = tg * 4 + tc
            for half in range(2):
                for fc in range(NFC):
                    inst = te.matmul(PQ[:, :], xn_sb[fc][:, tc * 128:(tc + 1) * 128],
                                     wq_sb[:, fc * D + half * TG: fc * D + (half + 1) * TG],
                                     start=(fc == 0), stop=(fc == NFC - 1))
                env.inc("amm", inst)
                env.wait("amm", v)
                env.inc("aev", rope_evac(tci, half, cq_sb, sq_sb, qstage))
                env.wait("aev", te)
                for fc in range(NFC):
                    inst = te.matmul(PQ[:, :], xn_sb[fc][:, tc * 128:(tc + 1) * 128],
                                     wk_sb[:, fc * D + half * TG: fc * D + (half + 1) * TG],
                                     start=(fc == 0), stop=(fc == NFC - 1))
                env.inc("amm", inst)
                env.wait("amm", v)
                env.inc("aev", rope_evac(tci, half, ck_sb, sk_sb, kstage))
                env.wait("aev", te)
                for fc in range(NFC):
                    inst = te.matmul(PQ[:, :], xn_sb[fc][:, tc * 128:(tc + 1) * 128],
                                     wv_sb[:, fc * D + half * TG: fc * D + (half + 1) * TG],
                                     start=(fc == 0), stop=(fc == NFC - 1))
                env.inc("amm", inst)
                env.wait("amm", v)
                vsl = vstage[:, 65 * 8 * half: 65 * 8 * (half + 1)].rearrange("p (h s) -> p h s", s=65)[:, :, 0:64]
                pv3 = PQ[:, :].rearrange("p (h s) -> p h s", s=64)
                env.inc("aev", v.tensor_copy(vsl, pv3))
                env.wait("aev", te)
            env.wait("aev", v)
            env.inc("aev", v.memset(vstage[:, 64:NH * 65:65], 1.0))
            env.wait("aev", sy)
            env.inc("ast", sy.dma_start(out=q_tm[tci * 128:(tci + 1) * 128, :], in_=qstage[:, :]), 16)
            env.inc("ast", sy.dma_start(out=k_tm[tci * 128:(tci + 1) * 128, :], in_=kstage[:, :]), 16)
            env.inc("ast", sy.dma_start(out=v_tm[tci * 128:(tci + 1) * 128, :], in_=vstage[:, :]), 16)
            env.wait("ast", v)
            env.wait("ast", te)
        env.barrier()
    env.wait("ast", sy)


def build_l1attn_a():
    nc = bass.Bass("TRN2", target_bir_lowering=False)
    env = Env(nc)
    io = dict(
        x_in=nc.dram_tensor("x_fm", [D, T], F32, kind="ExternalInput").ap(),
        wqt=nc.dram_tensor("wq_t", [NFC, NFC, 128, 128], BF16, kind="ExternalInput").ap(),
        wkt=nc.dram_tensor("wk_t", [NFC, NFC, 128, 128], BF16, kind="ExternalInput").ap(),
        wvt=nc.dram_tensor("wv_t", [NFC, NFC, 128, 128], BF16, kind="ExternalInput").ap(),
        cos_q=nc.dram_tensor("cos_q", [4096, 32], F32, kind="ExternalInput").ap(),
        sin_q=nc.dram_tensor("sin_q", [4096, 32], F32, kind="ExternalInput").ap(),
        cos_k=nc.dram_tensor("cos_k", [4096, 32], F32, kind="ExternalInput").ap(),
        sin_k=nc.dram_tensor("sin_k", [4096, 32], F32, kind="ExternalInput").ap(),
        q_tm=nc.dram_tensor("q_tm", [T, D], BF16, kind="ExternalOutput").ap(),
        k_tm=nc.dram_tensor("k_tm", [T, D], BF16, kind="ExternalOutput").ap(),
        v_tm=nc.dram_tensor("v_tm", [T, NH * 65], BF16, kind="ExternalOutput").ap(),
    )
    c = common_consts(env)
    emit_l1attn_a(env, c, io)
    env.close()
    return nc


def emit_l1attn_b(env, c, io, truncb=99):
    def bail():
        sy = env.nc.sync
        for fc in range(NFC):
            env.inc("xst", sy.dma_start(out=io["x_out"][fc * 128:(fc + 1) * 128, :],
                                        in_=io["x_in"][fc * 128:(fc + 1) * 128, :]), 16)
        env.wait("xst", sy)
    nc = env.nc
    POS = T * 16
    x_io, x_out = io["x_in"], io["x_out"]
    q_tm, k_tm, v_tm = io["q_tm"], io["k_tm"], io["v_tm"]
    wot, attn_d = io["wot"], io["attn_d"]
    ident, identf = c["ident"], c["identf"]
    q_pm = q_tm.rearrange("t (j d) -> (t j) d", d=64)
    k_pm = k_tm.rearrange("t (j d) -> (t j) d", d=64)
    v_pm = v_tm.rearrange("t (j e) -> (t j) e", e=65)

    te, v, se, sy, gp = nc.tensor, nc.vector, nc.scalar, nc.sync, nc.gpsimd
    if truncb <= -1:
        bail()
        return
    x_sb = [env.sbuf(f"x{fc}", [128, T], F32) for fc in range(NFC)]
    for fc in range(NFC):
        env.inc("cld", gp.dma_start(out=x_sb[fc][:, :], in_=x_io[fc * 128:(fc + 1) * 128, :]), 16)
    if truncb <= -3:
        env.wait("cld", gp)
        bail()
        return
    wo_sb = env.sbuf("wo_sb", [128, NFC * D], BF16)
    for fc in range(NFC):
        env.inc("cld", gp.dma_start(out=wo_sb[:, fc * D:(fc + 1) * D].rearrange("p (oc k) -> p oc k", k=128),
                                    in_=wot[fc, :, :, :].transpose([1, 0, 2])), 16)
    if truncb <= -2:
        env.wait("cld", gp)
        bail()
        return
    v_sb = env.sbuf("v_sb", [128, 8 * 32 * 65], BF16)
    for hl in range(8):
        env.inc("cld", sy.dma_start(
            out=v_sb[:, hl * 32 * 65:(hl + 1) * 32 * 65].rearrange("p (pc e) -> p pc e", e=65),
            in_=v_pm[hl * 4096:(hl + 1) * 4096, :].rearrange("(pc p) e -> p pc e", p=128)), 16)
    for e in (nc.vector, nc.scalar, nc.tensor, nc.sync):
        env.wait("cld", e)
    if truncb <= 0:
        bail()
        return

    qbuf = env.sbuf("qbuf", [128, D], BF16)
    sqr = env.sbuf("sqr", [128, D], BF16)
    qn = env.sbuf("qn", [128, 16], F32)
    nrm = env.sbuf("nrm", [128, 2], F32)
    nrmb = env.sbuf("nrmb", [128, 2], BF16)
    cstat = env.sbuf("cstat", [1, 8], F32)
    cneg_b = env.sbuf("cneg_b", [1, 2], BF16)
    cb = env.sbuf("cb", [128, 1], F32)
    PTB = env.psum("PTB", [128, 128], BF16)
    PTF = env.psum("PTF", [128, 128], F32)
    for which, src_t in [(0, q_tm), (1, k_tm)]:
        for tc in range(16):
            env.wait("nsq", gp)
            env.inc("nld", gp.dma_start(out=qbuf[:, :], in_=src_t[tc * 128:(tc + 1) * 128, 0:D]), 16)
            env.wait("nld", se)
            env.inc("nsq", se.activation(sqr[:, :], qbuf[:, :], AF.Square, accum_out=qn[:, tc:tc + 1]))
        env.wait("nsq", v)
        env.vsync(v.tensor_reduce(nrm[:, which:which + 1], qn[:, 0:16], AX.X, OP.max))
        env.inc("nstat", v.tensor_copy(nrmb[:, which:which + 1], nrm[:, which:which + 1]))
    env.wait("nstat", te)
    for which in range(2):
        env.inc("ntr", te.transpose(PTB[0:1, :], nrmb[:, which:which + 1], ident[:, :]))
        env.wait("ntr", v)
        env.vsync(v.tensor_reduce(cstat[0:1, which:which + 1], PTB[0:1, :], AX.X, OP.max))
        env.wait("_vch", te)
    env.wait("_vch", se)
    env.ssync(se.activation(cstat[0:1, 2:3], cstat[0:1, 0:1], AF.Sqrt))
    env.inc("nsq", se.activation(cstat[0:1, 3:4], cstat[0:1, 1:2], AF.Sqrt))
    env.wait("nsq", v)
    env.vsync(v.tensor_tensor(cstat[0:1, 4:5], cstat[0:1, 2:3], cstat[0:1, 3:4], OP.mult))
    env.vsync(v.tensor_scalar(cstat[0:1, 5:6], cstat[0:1, 4:5], -0.1275, None, OP.mult))
    env.inc("nstat", v.tensor_copy(cneg_b[0:1, 0:1], cstat[0:1, 5:6]))
    env.wait("nstat", te)
    env.inc("ntr", te.matmul(PTF[:, 0:1], c["ones_row_b"][0:1, :], cneg_b[0:1, 0:1], start=True, stop=True))
    env.wait("ntr", v)
    env.inc("nstat", v.tensor_copy(cb[:, :], PTF[:, 0:1]))
    env.barrier()
    if truncb <= 1:
        bail()
        return

    q_pos = [env.sbuf(f"qp{p}", [128, 4096], BF16) for p in range(4)]
    k_pos = [env.sbuf(f"kp{p}", [128, 4096], BF16) for p in range(4)]
    for src_pm, dst in [(q_pm, q_pos), (k_pm, k_pos)]:
        for hl in range(8):
            pr, rh = hl // 2, (hl % 2) * 64
            for pc in range(32):
                env.wait("bt_ld", te)
                env.wait("bt_tq", gp)
                env.inc("bt_ld", gp.dma_start(out=qbuf[:, 0:64],
                                              in_=src_pm[hl * 4096 + pc * 128: hl * 4096 + (pc + 1) * 128, :]), 16)
                env.wait("bt_ld", te)
                env.inc("bt_tr", te.transpose(PTB[0:64, :], qbuf[:, 0:64], ident[:, :]))
                env.wait("bt_tr", v)
                env.inc("bt_tq", v.tensor_copy(dst[pr][rh:rh + 64, pc * 128:(pc + 1) * 128], PTB[0:64, :]))
    env.barrier()
    if truncb <= 2:
        bail()
        return

    PSA = env.psum("PSA", [128, TG], F32)
    PSB = env.psum("PSB", [128, TG], F32)
    POA = env.psum("POA", [65, TG], F32)
    POB = env.psum("POB", [65, TG], F32)
    aT = env.sbuf("aT", [128, 2 * TG], BF16)
    poT = env.sbuf("poT", [65, TG], F32)
    astage = env.sbuf("astage", [128, 64], BF16)
    for qg in range(8):
        qsl = slice(qg * TG, (qg + 1) * TG)
        for pair in range(4):
            for kc in range(32):
                ksl = slice(kc * 128, (kc + 1) * 128)
                te.matmul(PSA[:, :], k_pos[pair][0:64, ksl], q_pos[pair][0:64, qsl], start=True, stop=True)
                inst = te.matmul(PSB[:, :], k_pos[pair][64:128, ksl], q_pos[pair][64:128, qsl],
                                 start=True, stop=True, tile_position=(64, 0))
                env.inc("smm", inst)
                env.wait("smm", se)
                env.ssync(se.activation(aT[:, 0:TG], PSA[:, :], AF.Exp, bias=cb[:, 0:1]))
                env.inc("sexp", se.activation(aT[:, TG:2 * TG], PSB[:, :], AF.Exp, bias=cb[:, 0:1]))
                env.wait("sexp", te)
                h0 = pair * 2
                te.matmul(POA[:, :], v_sb[:, (h0 * 32 + kc) * 65:(h0 * 32 + kc + 1) * 65], aT[:, 0:TG],
                          start=(kc == 0), stop=(kc == 31))
                inst = te.matmul(POB[:, :], v_sb[:, ((h0 + 1) * 32 + kc) * 65:((h0 + 1) * 32 + kc + 1) * 65],
                                 aT[:, TG:2 * TG], start=(kc == 0), stop=(kc == 31))
                env.inc("avmm", inst)
                env.wait("avmm", se)
            env.wait("avmm", v)
            for hh, PO in ((0, POA), (1, POB)):
                hl = pair * 2 + hh
                env.inc("dvv", v.tensor_copy(poT[:, :], PO[:, :]))
                env.wait("dvv", te)
                for sub in range(4):
                    env.wait("ep_done", te)
                    env.inc("eptr", te.transpose(PTF[:, 0:65], poT[:, sub * 128:(sub + 1) * 128],
                                                 identf[0:65, 0:65]))
                    env.wait("eptr", v)
                    env.vsync(v.reciprocal(qn[:, 2:3], PTF[:, 64:65]))
                    env.vsync(v.tensor_scalar(astage[:, :], PTF[:, 0:64], qn[:, 2:3], None, OP.mult))
                    env.wait("_vch", sy)
                    env.inc("ep_st", sy.dma_start(
                        out=attn_d[hl * 4096 + qg * 512 + sub * 128: hl * 4096 + qg * 512 + (sub + 1) * 128, :],
                        in_=astage[:, :]), 16)
                    env.inc("ep_done", v.engine_nop())
                    env.wait("ep_st", v)
        env.barrier()

    if truncb <= 3:
        bail()
        return
    attn_tv = attn_d.rearrange("(t j) d -> t (j d)", j=16)
    attn_fm = [env.sbuf(f"at{fc}", [128, TG], BF16) for fc in range(NFC)]
    PD = env.psum("PD", [128, TG], F32)
    for tg in range(NTG):
        sl = slice(tg * TG, (tg + 1) * TG)
        env.wait("omm", sy)
        for fc in range(NFC):
            for tc in range(4):
                env.inc("aft", sy.dma_start(out=attn_fm[fc][:, tc * 128:(tc + 1) * 128],
                                            in_=attn_tv[tg * TG + tc * 128: tg * TG + (tc + 1) * 128,
                                                        fc * 128:(fc + 1) * 128],
                                            transpose=True), 16)
        env.wait("aft", te)
        for oc in range(NFC):
            for fc in range(NFC):
                inst = te.matmul(PD[:, :], wo_sb[:, fc * D + oc * 128: fc * D + (oc + 1) * 128],
                                 attn_fm[fc][:, 0:TG], start=(fc == 0), stop=(fc == NFC - 1))
            env.inc("omm", inst)
            env.wait("omm", v)
            env.inc("oev", v.tensor_tensor(x_sb[oc][:, sl], x_sb[oc][:, sl], PD[:, :], OP.add))
            env.wait("oev", te)
    env.barrier()
    for fc in range(NFC):
        env.inc("xst", sy.dma_start(out=x_out[fc * 128:(fc + 1) * 128, :], in_=x_sb[fc][:, :]), 16)
    env.wait("xst", sy)


def build_l1attn_b():
    nc = bass.Bass("TRN2", target_bir_lowering=False)
    env = Env(nc)
    io = dict(
        x_in=nc.dram_tensor("x_fm", [D, T], F32, kind="ExternalInput").ap(),
        x_out=nc.dram_tensor("x_fm_out", [D, T], F32, kind="ExternalOutput").ap(),
        q_tm=nc.dram_tensor("q_tm", [T, D], BF16, kind="ExternalInput").ap(),
        k_tm=nc.dram_tensor("k_tm", [T, D], BF16, kind="ExternalInput").ap(),
        v_tm=nc.dram_tensor("v_tm", [T, NH * 65], BF16, kind="ExternalInput").ap(),
        wot=nc.dram_tensor("wo_t", [NFC, NFC, 128, 128], BF16, kind="ExternalInput").ap(),
        attn_d=nc.dram_tensor("attn_d", [T * 16, 64], BF16, kind="Internal").ap(),
    )
    idb = nc.dram_tensor("id_bf", [128, 128], BF16, kind="ExternalInput").ap()
    idf = nc.dram_tensor("id_f32", [128, 128], F32, kind="ExternalInput").ap()
    c = common_consts(env)
    load_idents(env, c, idb, idf)
    emit_l1attn_b(env, c, io)
    env.close()
    return nc



def emit_l0attn_a(env, c, io):
    nc = env.nc
    x_io = io["x_in"]
    qkvt, wgt = io["qkvt"], io["wgt"]
    bq_c, bg_c, bkv_r = io["bq_c"], io["bg_c"], io["bkv_r"]
    q_fm_d, g_fm_d, k_tm, v_tm = io["q_fm"], io["g_fm"], io["k_tm"], io["v_tm"]

    te, v, se, sy, gp = nc.tensor, nc.vector, nc.scalar, nc.sync, nc.gpsimd

    x_sb = [env.sbuf(f"x{fc}", [128, T], F32) for fc in range(NFC)]
    xn_sb = [env.sbuf(f"xn{fc}", [128, TG], BF16) for fc in range(NFC)]
    for fc in range(NFC):
        env.inc("cld", gp.dma_start(out=x_sb[fc][:, :], in_=x_io[fc * 128:(fc + 1) * 128, :]), 16)
    qkv_sb = env.sbuf("qkv_sb", [128, NFC * 24 * 128], BF16)
    wg_sb = env.sbuf("wg_sb", [128, NFC * D], BF16)
    for fc in range(NFC):
        env.inc("cld", gp.dma_start(out=qkv_sb[:, fc * 3072:(fc + 1) * 3072].rearrange("p (oc k) -> p oc k", k=128),
                                    in_=qkvt[fc, :, :, :].transpose([1, 0, 2])), 16)
        env.inc("cld", gp.dma_start(out=wg_sb[:, fc * D:(fc + 1) * D].rearrange("p (oc k) -> p oc k", k=128),
                                    in_=wgt[fc, :, :, :].transpose([1, 0, 2])), 16)
    bq_sb = env.sbuf("bq_sb", [128, 8], F32)
    bq8 = env.sbuf("bq8", [128, 8], F32)
    bg_sb = env.sbuf("bg_sb", [128, 8], F32)
    bkv_sb = env.sbuf("bkv_sb", [1, 2048], BF16)
    env.inc("cld", gp.dma_start(out=bq_sb[:, :], in_=bq_c), 16)
    env.inc("cld", gp.dma_start(out=bg_sb[:, :], in_=bg_c), 16)
    env.inc("cld", gp.dma_start(out=bkv_sb[:, :], in_=bkv_r), 16)
    for e in (nc.vector, nc.scalar, nc.tensor, nc.sync):
        env.wait("cld", e)
    env.inc("bq8d", v.tensor_scalar(bq8[:, :], bq_sb[:, :], 0.125, None, OP.mult))
    env.wait("bq8d", se)

    PA = env.psum("PA", [128, TG], F32)
    PB = env.psum("PB", [128, TG], F32)
    PQ = env.psum("PQ", [128, TG], F32)
    lns = ln_scratch(env, PA, PB)
    ostage = env.sbuf("ostage", [128, TG], BF16)
    vstage = env.sbuf("vstage", [128, NH * 65], BF16)
    kstage = env.sbuf("kstage", [128, D], BF16)

    for tg in range(NTG):
        sl = slice(tg * TG, (tg + 1) * TG)
        emit_ln_fm(env, c, x_sb, tg, xn_sb, lns)
        env.barrier()
        for kind, wsb, stride, bias, func, outd in (
                ("q", qkv_sb, 3072, bq8, AF.Identity, q_fm_d),
                ("g", wg_sb, 1024, bg_sb, AF.Sigmoid, g_fm_d)):
            for oc in range(NFC):
                for fc in range(NFC):
                    inst = te.matmul(PQ[:, :], wsb[:, fc * stride + oc * 128: fc * stride + (oc + 1) * 128],
                                     xn_sb[fc][:, 0:TG], start=(fc == 0), stop=(fc == NFC - 1))
                env.inc("pmm", inst)
                env.wait("pmm", se)
                if kind == "q":
                    env.inc("pev", se.activation(ostage[:, :], PQ[:, :], func, bias=bias[:, oc:oc + 1], scale=0.125))
                else:
                    env.inc("pev", se.activation(ostage[:, :], PQ[:, :], func, bias=bias[:, oc:oc + 1]))
                env.wait("pev", sy)
                env.inc("pst", sy.dma_start(out=outd[oc * 128:(oc + 1) * 128, sl], in_=ostage[:, :]), 16)
                env.wait("pst", se)
                env.wait("pst", te)
        for tc in range(4):
            tci = tg * 4 + tc
            for base_oc, stage, kind in ((8, kstage, "k"), (16, vstage, "v")):
                for half in range(2):
                    for fc in range(NFC):
                        te.matmul(PQ[:, :], xn_sb[fc][:, tc * 128:(tc + 1) * 128],
                                  qkv_sb[:, (fc * 24 + base_oc + half * 4) * 128: (fc * 24 + base_oc + (half + 1) * 4) * 128],
                                  start=(fc == 0), stop=False)
                    inst = te.matmul(PQ[:, :], c["ones_row_b"][0:1, :],
                                     bkv_sb[0:1, (base_oc - 8) * 128 + half * TG: (base_oc - 8) * 128 + (half + 1) * TG],
                                     start=False, stop=True)
                    env.inc("pmm", inst)
                    env.wait("pmm", v)
                    if kind == "k":
                        env.inc("pev2", v.tensor_copy(stage[:, half * TG:(half + 1) * TG], PQ[:, :]))
                    else:
                        vsl = stage[:, 65 * 8 * half: 65 * 8 * (half + 1)].rearrange("p (h s) -> p h s", s=65)[:, :, 0:64]
                        env.inc("pev2", v.tensor_copy(vsl, PQ[:, :].rearrange("p (h s) -> p h s", s=64)))
                    env.wait("pev2", te)
            env.wait("pev2", v)
            env.inc("pev2", v.memset(vstage[:, 64:NH * 65:65], 1.0))
            env.wait("pev2", sy)
            env.inc("pst", sy.dma_start(out=k_tm[tci * 128:(tci + 1) * 128, :], in_=kstage[:, :]), 16)
            env.inc("pst", sy.dma_start(out=v_tm[tci * 128:(tci + 1) * 128, :], in_=vstage[:, :]), 16)
            env.wait("pst", v)
            env.wait("pst", te)
        env.barrier()
    env.wait("pst", sy)


def build_l0attn_a():
    nc = bass.Bass("TRN2", target_bir_lowering=False)
    env = Env(nc)
    io = dict(
        x_in=nc.dram_tensor("x_fm", [D, T], F32, kind="ExternalInput").ap(),
        qkvt=nc.dram_tensor("qkv_t", [NFC, 24, 128, 128], BF16, kind="ExternalInput").ap(),
        wgt=nc.dram_tensor("wgate_t", [NFC, NFC, 128, 128], BF16, kind="ExternalInput").ap(),
        bq_c=nc.dram_tensor("bq_c", [128, 8], F32, kind="ExternalInput").ap(),
        bg_c=nc.dram_tensor("bg_c", [128, 8], F32, kind="ExternalInput").ap(),
        bkv_r=nc.dram_tensor("bkv_r", [1, 2048], F32, kind="ExternalInput").ap(),
        q_fm=nc.dram_tensor("q_fm", [D, T], BF16, kind="ExternalOutput").ap(),
        g_fm=nc.dram_tensor("g_fm", [D, T], BF16, kind="ExternalOutput").ap(),
        k_tm=nc.dram_tensor("k_tm", [T, D], BF16, kind="ExternalOutput").ap(),
        v_tm=nc.dram_tensor("v_tm", [T, NH * 65], BF16, kind="ExternalOutput").ap(),
    )
    c = common_consts(env)
    emit_l0attn_a(env, c, io)
    env.close()
    return nc



def emit_exchange(env, c, io):
    nc = env.nc
    sy, gp, v = nc.sync, nc.gpsimd, nc.vector
    z = env.sbuf("zrow", [128, NH * 65], BF16)
    env.inc("zpr", v.memset(z[:, :], 0.0))
    env.wait("zpr", sy)
    for i in range(MAXOFF // 128):
        env.inc("zst", sy.dma_start(out=io["kpad"][i * 128:(i + 1) * 128, :], in_=z[:, 0:D]), 16)
        env.inc("zst", sy.dma_start(out=io["vpad"][i * 128:(i + 1) * 128, :], in_=z[:, :]), 16)
    env.inc("cc", gp.collective_compute("AllGather", OP.bypass, ins=[io["k_tm"]],
                                        outs=[io["kgath"]], replica_groups=PAIRS), 1)
    env.inc("cc", gp.collective_compute("AllGather", OP.bypass, ins=[io["v_tm"]],
                                        outs=[io["vgath"]], replica_groups=PAIRS), 1)
    env.wait("cc", sy)
    env.wait("cc", gp)
    CH = T // 4
    for i in range(8):
        eng = sy if i % 2 == 0 else gp
        env.inc("xpd", eng.dma_start(out=io["kpad"][MAXOFF + i * CH: MAXOFF + (i + 1) * CH, :],
                                     in_=io["kgath"][i * CH:(i + 1) * CH, :]), 16)
        env.inc("xpd", eng.dma_start(out=io["vpad"][MAXOFF + i * CH: MAXOFF + (i + 1) * CH, :],
                                     in_=io["vgath"][i * CH:(i + 1) * CH, :]), 16)
    env.wait("zst", sy)
    env.wait("xpd", sy)
    env.wait("zst", gp)
    env.wait("xpd", gp)
    pid_s = sy.partition_id()
    pid_g = gp.partition_id()
    for i in range(WIN // 512):
        env.inc("xwn", sy.dma_start(out=io["k_win"][i * 512:(i + 1) * 512, :],
                                    in_=io["kpad"][ds((pid_s & 1) * T + i * 512, 512), :]), 16)
        env.inc("xwn", gp.dma_start(out=io["v_win"][i * 512:(i + 1) * 512, :],
                                    in_=io["vpad"][ds((pid_g & 1) * T + i * 512, 512), :]), 16)
    env.wait("xwn", sy)
    env.wait("xwn", gp)


def emit_l0attn_b(env, c, io):
    nc = env.nc
    x_io, x_out = io["x_in"], io["x_out"]
    q_fm_d, g_fm_d = io["q_fm"], io["g_fm"]
    k_win, v_win = io["k_win"], io["v_win"]
    wot, bo_c, pb_t, flat_d = io["wout"], io["bo_c"], io["pb_t"], io["flat_d"]
    ident, identf = c["ident"], c["identf"]

    te, v, se, sy, gp = nc.tensor, nc.vector, nc.scalar, nc.sync, nc.gpsimd
    wo_sb = env.sbuf("wo_sb", [128, NFC * D], BF16)
    for fc in range(NFC):
        env.inc("cld", gp.dma_start(out=wo_sb[:, fc * D:(fc + 1) * D].rearrange("p (oc k) -> p oc k", k=128),
                                    in_=wot[fc, :, :, :].transpose([1, 0, 2])), 16)
    q_sb = [env.sbuf(f"qf{fc}", [128, T], BF16) for fc in range(NFC)]
    for fc in range(NFC):
        env.inc("cld", gp.dma_start(out=q_sb[fc][:, :], in_=q_fm_d[fc * 128:(fc + 1) * 128, :]), 16)
    bo_sb = env.sbuf("bo_sb", [128, 8], F32)
    env.inc("cld", gp.dma_start(out=bo_sb[:, :], in_=bo_c), 16)
    for e in (nc.vector, nc.scalar, nc.tensor, nc.sync):
        env.wait("cld", e)

    PTB = env.psum("PTB", [1, 128], BF16)
    PTF = env.psum("PTF", [128, 128], F32)
    pbe_sb = env.sbuf("pbe_sb", [128, ND * NH * 128], BF16)
    pbld = env.sbuf("pbld", [128, 2 * 128], F32)
    pmx = env.sbuf("pmx", [128, ND * NH + 2], F32)
    cstat = env.sbuf("cstat", [1, 8], F32)
    cn_b = env.sbuf("cn_b", [1, 2], BF16)
    cb = env.sbuf("cb", [128, 1], F32)
    mpb = env.sbuf("mpb", [128, 1], F32)
    for i in range(ND * NH):
        d_, h_ = i // NH, i % NH
        b = i % 2
        if i >= 2:
            env.wait_val("pbm", gp, env.val("pbm") - 1)
        env.inc("pbl", gp.dma_start(out=pbld[:, b * 128:(b + 1) * 128], in_=pb_t[d_, h_, :, :]), 16)
        env.wait("pbl", v)
        env.inc("pbm", v.tensor_reduce(pmx[:, i:i + 1], pbld[:, b * 128:(b + 1) * 128], AX.X, OP.max))
    env.wait("pbm", v)
    env.vsync(v.tensor_reduce(pmx[:, ND * NH:ND * NH + 1], pmx[:, 0:ND * NH], AX.X, OP.max))
    pmxb = env.sbuf("pmxb", [128, 1], BF16)
    env.vsync(v.tensor_copy(pmxb[:, :], pmx[:, ND * NH:ND * NH + 1]))
    env.wait("_vch", te)
    env.inc("pbtr", te.transpose(PTB[:, :], pmxb[:, 0:1], ident[:, :]))
    env.wait("pbtr", v)
    env.vsync(v.tensor_reduce(cstat[0:1, 0:1], PTB[0:1, :], AX.X, OP.max))
    env.vsync(v.tensor_copy(cn_b[0:1, 0:1], cstat[0:1, 0:1]))
    env.wait("_vch", te)
    env.inc("pbtr", te.matmul(PTF[:, 0:1], c["ones_row_b"][0:1, :], cn_b[0:1, 0:1], start=True, stop=True))
    env.wait("pbtr", v)
    env.vsync(v.tensor_scalar(mpb[:, :], PTF[:, 0:1], -1.0, None, OP.mult))
    env.wait("_vch", se)
    for i in range(ND * NH):
        d_, h_ = i // NH, i % NH
        b = i % 2
        env.wait_val("pbe", gp, max(0, env.val("pbe") - 1))
        env.inc("pbl", gp.dma_start(out=pbld[:, b * 128:(b + 1) * 128], in_=pb_t[d_, h_, :, :]), 16)
        env.wait("pbl", se)
        env.inc("pbe", se.activation(pbe_sb[:, i * 128:(i + 1) * 128], pbld[:, b * 128:(b + 1) * 128],
                                     AF.Exp, bias=mpb[:, 0:1]))
    env.barrier()

    qbuf = env.sbuf("qbuf", [128, D], BF16)
    sqr = env.sbuf("sqr", [128, D], BF16)
    qn = env.sbuf("qn", [128, NWC], F32)
    nrm = env.sbuf("nrm", [128, 2], F32)
    nrmb = env.sbuf("nrmb", [128, 2], BF16)
    for tc in range(NWC):
        env.wait("nsq", gp)
        env.inc("nld", gp.dma_start(out=qbuf[:, :], in_=k_win[tc * 128:(tc + 1) * 128, 0:D]), 16)
        env.wait("nld", se)
        env.inc("nsq", se.activation(sqr[:, :], qbuf[:, :], AF.Square, accum_out=qn[:, tc:tc + 1]))
    env.wait("nsq", v)
    env.vsync(v.tensor_reduce(nrm[:, 1:2], qn[:, 0:NWC], AX.X, OP.max))
    env.inc("nstat", v.tensor_copy(nrmb[:, 1:2], nrm[:, 1:2]))
    PSA = env.psum("PSA", [128, 128], F32)
    PSB = env.psum("PSB", [128, 128], F32)
    PS1 = env.psum("PS1", [1, TG], F32)
    sqf = env.sbuf("sqf", [128, TG], F32)
    for tgi in range(NTG):
        env.wait("nstat", te)
        for fc in range(NFC):
            if fc >= 1:
                env.wait("nmm", se)
            env.inc("nsq", se.activation(sqf[:, :], q_sb[fc][:, tgi * TG:(tgi + 1) * TG], AF.Square))
            env.wait("nsq", te)
            env.inc("nmm", te.matmul(PS1[:, :], c["ones_col_f"][:, :], sqf[:, :],
                                     start=(fc == 0), stop=(fc == NFC - 1)))
        env.wait("nmm", v)
        env.inc("nstat", v.tensor_reduce(qn[0:1, 20 + tgi:21 + tgi], PS1[:, :], AX.X, OP.max))
    env.wait("nstat", v)
    env.vsync(v.tensor_reduce(nrm[0:1, 0:1], qn[0:1, 20:20 + NTG], AX.X, OP.max))
    env.inc("nstat", v.tensor_copy(nrmb[0:1, 0:1], nrm[0:1, 0:1]))
    env.wait("nstat", te)
    env.inc("pbtr", te.transpose(PTB[:, :], nrmb[:, 1:2], ident[:, :]))
    env.wait("pbtr", v)
    env.vsync(v.tensor_reduce(cstat[0:1, 1:2], PTB[0:1, :], AX.X, OP.max))
    env.vsync(v.tensor_copy(cstat[0:1, 0:1], nrmb[0:1, 0:1]))
    env.wait("_vch", se)
    env.ssync(se.activation(cstat[0:1, 2:3], cstat[0:1, 0:1], AF.Sqrt))
    env.inc("nsq", se.activation(cstat[0:1, 3:4], cstat[0:1, 1:2], AF.Sqrt))
    env.wait("nsq", v)
    env.vsync(v.tensor_tensor(cstat[0:1, 4:5], cstat[0:1, 2:3], cstat[0:1, 3:4], OP.mult))
    env.vsync(v.tensor_scalar(cstat[0:1, 5:6], cstat[0:1, 4:5], -1.02, None, OP.mult))
    env.inc("nstat", v.tensor_copy(cn_b[0:1, 0:1], cstat[0:1, 5:6]))
    env.wait("nstat", te)
    env.inc("pbtr", te.matmul(PTF[:, 0:1], c["ones_row_b"][0:1, :], cn_b[0:1, 0:1], start=True, stop=True))
    env.wait("pbtr", v)
    env.vsync(v.tensor_tensor(cb[:, :], PTF[:, 0:1], mpb[:, :], OP.add))
    env.barrier()

    k_fm = [env.sbuf(f"kf{p}", [128, WIN], BF16) for p in range(NFC)]
    for p in range(NFC):
        for w in range(NWC):
            env.inc("kft", sy.dma_start(out=k_fm[p][:, w * 128:(w + 1) * 128],
                                        in_=k_win[w * 128:(w + 1) * 128, p * 128:(p + 1) * 128],
                                        transpose=True), 16)
    env.wait("kft", te)
    env.barrier()

    POA = env.psum("POA", [65, 128], F32)
    POB = env.psum("POB", [65, 128], F32)
    vta = [env.sbuf(f"vt{i}", [128, 130], BF16) for i in range(2)]
    eT = env.sbuf("eT", [128, 2 * 128], BF16)
    aT = env.sbuf("aT", [128, 2 * 128], BF16)
    poT = env.sbuf("poT", [65, 128], F32)
    astage = env.sbuf("astage", [128, 64], BF16)
    for qt in range(16):
        qsl = slice(qt * 128, (qt + 1) * 128)
        for pair in range(NFC):
            h0 = 2 * pair
            avmm0 = env.val("avmm")
            for d_ in range(ND):
                w = qt - (ND - 1 - d_) + 12
                dist = ND - 1 - d_
                buf = d_ % 2
                env.wait_val("avmm", gp, avmm0 + d_ - 1 if d_ >= 2 else avmm0)
                env.inc("kld", gp.dma_start(out=vta[buf][:, 0:130],
                                            in_=v_win[w * 128:(w + 1) * 128, 65 * h0: 65 * h0 + 130]), 16)
                env.wait("kld", te)
                wsl = slice(w * 128, (w + 1) * 128)
                te.matmul(PSA[:, :], k_fm[pair][0:64, wsl], q_sb[pair][0:64, qsl], start=True, stop=True)
                inst = te.matmul(PSB[:, :], k_fm[pair][64:128, wsl], q_sb[pair][64:128, qsl],
                                 start=True, stop=True, tile_position=(64, 0))
                env.inc("smm", inst)
                env.wait("smm", se)
                env.ssync(se.activation(eT[:, 0:128], PSA[:, :], AF.Exp, bias=cb[:, 0:1]))
                env.inc("sexp", se.activation(eT[:, 128:256], PSB[:, :], AF.Exp, bias=cb[:, 0:1]))
                env.wait("sexp", v)
                env.vsync(v.tensor_tensor(aT[:, 0:128], eT[:, 0:128],
                                          pbe_sb[:, (dist * NH + h0) * 128:(dist * NH + h0 + 1) * 128], OP.mult))
                env.inc("aTd", v.tensor_tensor(aT[:, 128:256], eT[:, 128:256],
                                               pbe_sb[:, (dist * NH + h0 + 1) * 128:(dist * NH + h0 + 2) * 128], OP.mult))
                env.wait("aTd", te)
                te.matmul(POA[:, :], vta[buf][:, 0:65], aT[:, 0:128], start=(d_ == 0), stop=(d_ == ND - 1))
                inst = te.matmul(POB[:, :], vta[buf][:, 65:130], aT[:, 128:256],
                                 start=(d_ == 0), stop=(d_ == ND - 1))
                env.inc("avmm", inst)
                env.wait("avmm", v)
            env.wait("avmm", v)
            for hh, PO in ((0, POA), (1, POB)):
                hl = h0 + hh
                env.inc("dvv", v.tensor_copy(poT[:, :], PO[:, :]))
                env.wait("dvv", te)
                env.wait("ep_done", te)
                env.inc("eptr", te.transpose(PTF[:, 0:65], poT[:, :], identf[0:65, 0:65]))
                env.wait("eptr", v)
                env.vsync(v.reciprocal(qn[:, 2:3], PTF[:, 64:65]))
                env.vsync(v.tensor_scalar(astage[:, :], PTF[:, 0:64], qn[:, 2:3], None, OP.mult))
                env.wait("_vch", sy)
                env.inc("ep_st", sy.dma_start(out=flat_d[qt * 128:(qt + 1) * 128, hl * 64:(hl + 1) * 64],
                                              in_=astage[:, :]), 16)
                env.inc("ep_done", v.engine_nop())
                env.wait("ep_st", v)
    env.barrier()

    PD = env.psum("PD", [128, TG], F32)
    fg_sb = [env.sbuf(f"fg{fc}", [128, TG], BF16) for fc in range(NFC)]
    gbuf = env.sbuf("gbuf", [128, TG], BF16)
    xres = env.sbuf("xres", [128, TG], F32)
    for tg in range(NTG):
        sl = slice(tg * TG, (tg + 1) * TG)
        env.wait("omm", sy)
        env.wait("omm", gp)
        for fc in range(NFC):
            for tc in range(4):
                env.inc("aft", sy.dma_start(out=fg_sb[fc][:, tc * 128:(tc + 1) * 128],
                                            in_=flat_d[tg * TG + tc * 128: tg * TG + (tc + 1) * 128,
                                                       fc * 128:(fc + 1) * 128],
                                            transpose=True), 16)
        for fc in range(NFC):
            env.inc("gld", gp.dma_start(out=gbuf[:, :], in_=g_fm_d[fc * 128:(fc + 1) * 128, sl]), 16)
            env.wait("gld", v)
            env.wait("aft", v)
            env.inc("fgd", v.tensor_tensor(fg_sb[fc][:, 0:TG], fg_sb[fc][:, 0:TG], gbuf[:, :], OP.mult))
            env.wait("fgd", gp)
        env.wait("fgd", te)
        for oc in range(NFC):
            env.wait("oev", gp)
            env.inc("xld", gp.dma_start(out=xres[:, :], in_=x_io[oc * 128:(oc + 1) * 128, sl]), 16)
            for fc in range(NFC):
                inst = te.matmul(PD[:, :], wo_sb[:, fc * D + oc * 128: fc * D + (oc + 1) * 128],
                                 fg_sb[fc][:, 0:TG], start=(fc == 0), stop=(fc == NFC - 1))
            env.inc("omm", inst)
            env.wait("omm", v)
            env.wait("xld", v)
            env.vsync(v.scalar_tensor_tensor(xres[:, :], PD[:, :], bo_sb[:, oc:oc + 1],
                                             xres[:, :], OP.add, OP.add))
            env.wait("_vch", sy)
            env.inc("oev", sy.dma_start(out=x_out[oc * 128:(oc + 1) * 128, sl], in_=xres[:, :]), 16)
            env.wait("oev", te)
    env.wait("oev", sy)


def build_l0attn_b():
    nc = bass.Bass("TRN2", target_bir_lowering=False)
    env = Env(nc)
    io = dict(
        x_in=nc.dram_tensor("x_fm", [D, T], F32, kind="ExternalInput").ap(),
        x_out=nc.dram_tensor("x_fm_out", [D, T], F32, kind="ExternalOutput").ap(),
        q_fm=nc.dram_tensor("q_fm", [D, T], BF16, kind="ExternalInput").ap(),
        g_fm=nc.dram_tensor("g_fm", [D, T], BF16, kind="ExternalInput").ap(),
        k_win=nc.dram_tensor("k_win", [WIN, D], BF16, kind="ExternalInput").ap(),
        v_win=nc.dram_tensor("v_win", [WIN, NH * 65], BF16, kind="ExternalInput").ap(),
        wout=nc.dram_tensor("wout_t", [NFC, NFC, 128, 128], BF16, kind="ExternalInput").ap(),
        bo_c=nc.dram_tensor("bo_c", [128, 8], F32, kind="ExternalInput").ap(),
        pb_t=nc.dram_tensor("pb_t", [ND, NH, 128, 128], F32, kind="ExternalInput").ap(),
        flat_d=nc.dram_tensor("flat_d", [T, D], BF16, kind="Internal").ap(),
    )
    idb = nc.dram_tensor("id_bf", [128, 128], BF16, kind="ExternalInput").ap()
    idf = nc.dram_tensor("id_f32", [128, 128], F32, kind="ExternalInput").ap()
    c = common_consts(env)
    load_idents(env, c, idb, idf)
    emit_l0attn_b(env, c, io)
    env.close()
    return nc



def build_fused(upto=9):
    nc = bass.Bass("TRN2", target_bir_lowering=False)
    env = Env(nc)
    ins = {}
    caches = {}
    for nm, cname, O, F, _ in PREP_WEIGHTS:
        ins[nm] = nc.dram_tensor(nm, [O, F], F32, kind="ExternalInput").ap()
        caches[cname] = nc.dram_tensor(cname, [F // 128, O // 128, 128, 128], BF16, kind="Internal").ap()
    ins["L1_ffn_k0"] = nc.dram_tensor("L1_ffn_k0", [2 * D, 4096], U8, kind="ExternalInput").ap()
    caches["beff_t"] = nc.dram_tensor("beff_t", [32, 8, 128, 128], BF16, kind="Internal").ap()

    x_in = nc.dram_tensor("x_fm", [D, T], F32, kind="ExternalInput").ap()
    x_out = nc.dram_tensor("x_fm_out", [D, T], F32, kind="ExternalOutput").ap()
    x_res = nc.dram_tensor("x_res", [D, T], F32, kind="Internal").ap()
    k1t = nc.dram_tensor("k1_t", [32, 32, 128, 128], U8, kind="ExternalInput").ap()
    k2t = nc.dram_tensor("k2_t", [NFC, 32, 128, 128], U8, kind="ExternalInput").ap()
    s_in = nc.dram_tensor("s_sc", [3, 1], F32, kind="ExternalInput").ap()
    bq_c = nc.dram_tensor("bq_c", [128, 8], F32, kind="ExternalInput").ap()
    bg_c = nc.dram_tensor("bg_c", [128, 8], F32, kind="ExternalInput").ap()
    bkv_r = nc.dram_tensor("bkv_r", [1, 2048], F32, kind="ExternalInput").ap()
    bo_c = nc.dram_tensor("bo_c", [128, 8], F32, kind="ExternalInput").ap()
    pb_t = nc.dram_tensor("pb_t", [ND, NH, 128, 128], F32, kind="ExternalInput").ap()
    cos_q = nc.dram_tensor("cos_q", [4096, 32], F32, kind="ExternalInput").ap()
    sin_q = nc.dram_tensor("sin_q", [4096, 32], F32, kind="ExternalInput").ap()
    cos_k = nc.dram_tensor("cos_k", [4096, 32], F32, kind="ExternalInput").ap()
    sin_k = nc.dram_tensor("sin_k", [4096, 32], F32, kind="ExternalInput").ap()
    idb = nc.dram_tensor("id_bf", [128, 128], BF16, kind="ExternalInput").ap()
    idf = nc.dram_tensor("id_f32", [128, 128], F32, kind="ExternalInput").ap()

    q_fm0 = nc.dram_tensor("q_fm0", [D, T], BF16, kind="Internal").ap()
    g_fm0 = nc.dram_tensor("g_fm0", [D, T], BF16, kind="Internal").ap()
    k_tm0 = nc.dram_tensor("k_tm0", [T, D], BF16, kind="Internal").ap()
    v_tm0 = nc.dram_tensor("v_tm0", [T, NH * 65], BF16, kind="Internal").ap()
    kgath = nc.dram_tensor("kgath", [2 * T, D], BF16, kind="Internal").ap()
    vgath = nc.dram_tensor("vgath", [2 * T, NH * 65], BF16, kind="Internal").ap()
    kpad = nc.dram_tensor("kpad", [MAXOFF + 2 * T, D], BF16, kind="Internal").ap()
    vpad = nc.dram_tensor("vpad", [MAXOFF + 2 * T, NH * 65], BF16, kind="Internal").ap()
    k_win = nc.dram_tensor("k_win", [WIN, D], BF16, kind="Internal").ap()
    v_win = nc.dram_tensor("v_win", [WIN, NH * 65], BF16, kind="Internal").ap()
    flat_d = nc.dram_tensor("flat_d", [T, D], BF16, kind="Internal").ap()
    q_tm1 = nc.dram_tensor("q_tm1", [T, D], BF16, kind="Internal").ap()
    k_tm1 = nc.dram_tensor("k_tm1", [T, D], BF16, kind="Internal").ap()
    v_tm1 = nc.dram_tensor("v_tm1", [T, NH * 65], BF16, kind="Internal").ap()
    attn_d = nc.dram_tensor("attn_d", [T * 16, 64], BF16, kind="Internal").ap()

    c = common_consts(env)
    load_idents(env, c, idb, idf)

    env.push("pp_")
    emit_prep(env, c, ins, caches)
    env.barrier()
    env.pop()

    def ffn0(tag, xi, xo):
        env.push(tag)
        emit_l0ffn(env, c, dict(x_in=xi, x_out=xo, wgq=caches["wgq_t"], wuq=caches["wuq_t"],
                                wdq=caches["wdq_t"]))
        env.barrier()
        env.pop()

    def ffn1(tag, xi, xo):
        env.push(tag)
        emit_l1ffn(env, c, dict(x_in=xi, x_out=xo, beff=caches["beff_t"], k1t=k1t, k2t=k2t, s_in=s_in))
        env.barrier()
        env.pop()

    done = [False]

    def fin():
        if done[0]:
            return
        done[0] = True
        sy = nc.sync
        env.push("fin_")
        for fc in range(NFC):
            env.inc("fxs", sy.dma_start(out=x_out[fc * 128:(fc + 1) * 128, :],
                                        in_=x_res[fc * 128:(fc + 1) * 128, :]), 16)
        env.wait("fxs", sy)
        env.pop()
        env.close()

    if upto == 80:
        env.push("a1_")
        emit_l1attn_a(env, c, dict(x_in=x_in, wqt=caches["wq_t"], wkt=caches["wk_t"],
                                   wvt=caches["wv_t"], cos_q=cos_q, sin_q=sin_q, cos_k=cos_k,
                                   sin_k=sin_k, q_tm=q_tm1, k_tm=k_tm1, v_tm=v_tm1))
        env.barrier()
        env.pop()
        env.push("b1_")
        import os as _os
        emit_l1attn_b(env, c, dict(x_in=x_in, x_out=x_res, q_tm=q_tm1, k_tm=k_tm1, v_tm=v_tm1,
                                   wot=caches["wo_t"], attn_d=attn_d),
                      truncb=int(_os.environ.get("TRUNCB", "99")))
        env.barrier()
        env.pop()
        fin()
        return nc

    ffn0("f0_", x_in, x_res)
    if upto <= 1:
        fin()
        return nc

    env.push("a0_")
    emit_l0attn_a(env, c, dict(x_in=x_res, qkvt=caches["qkv_t"], wgt=caches["wgate_t"],
                               bq_c=bq_c, bg_c=bg_c, bkv_r=bkv_r,
                               q_fm=q_fm0, g_fm=g_fm0, k_tm=k_tm0, v_tm=v_tm0))
    env.barrier()
    env.pop()
    if upto <= 2:
        fin()
        return nc

    env.push("x0_")
    emit_exchange(env, c, dict(k_tm=k_tm0, v_tm=v_tm0, kgath=kgath, vgath=vgath,
                               kpad=kpad, vpad=vpad, k_win=k_win, v_win=v_win))
    env.barrier()
    env.pop()
    if upto <= 3:
        fin()
        return nc

    env.push("b0_")
    emit_l0attn_b(env, c, dict(x_in=x_res, x_out=x_res, q_fm=q_fm0, g_fm=g_fm0,
                               k_win=k_win, v_win=v_win, wout=caches["wout_t"],
                               bo_c=bo_c, pb_t=pb_t, flat_d=flat_d))
    env.barrier()
    env.pop()
    if upto <= 4:
        fin()
        return nc

    ffn0("f1_", x_res, x_res)
    if upto <= 5:
        fin()
        return nc
    ffn1("g0_", x_res, x_res)
    if upto <= 6:
        fin()
        return nc

    env.push("a1_")
    emit_l1attn_a(env, c, dict(x_in=x_res, wqt=caches["wq_t"], wkt=caches["wk_t"], wvt=caches["wv_t"],
                               cos_q=cos_q, sin_q=sin_q, cos_k=cos_k, sin_k=sin_k,
                               q_tm=q_tm1, k_tm=k_tm1, v_tm=v_tm1))
    env.barrier()
    env.pop()
    if upto <= 7:
        fin()
        return nc

    env.push("b1_")
    emit_l1attn_b(env, c, dict(x_in=x_res, x_out=x_res, q_tm=q_tm1, k_tm=k_tm1, v_tm=v_tm1,
                               wot=caches["wo_t"], attn_d=attn_d))
    env.barrier()
    env.pop()
    if upto <= 8:
        fin()
        return nc

    ffn1("g1_", x_res, x_out)
    env.close()
    return nc

N_CORES = 8
_ID_BF = None
_ID_F32 = None


def _ids():
    global _ID_BF, _ID_F32
    if _ID_BF is None:
        _ID_BF = np.eye(128, dtype=ml_dtypes.bfloat16)
        _ID_F32 = np.eye(128, dtype=np.float32)
    return _ID_BF, _ID_F32


def _perm_evod():
    idx = []
    for h in range(16):
        idx += [h * 64 + 2 * i for i in range(32)]
        idx += [h * 64 + 2 * i + 1 for i in range(32)]
    return np.array(idx)


def _rope_tables(scale):
    inv = 1.0 / 10000.0 ** (np.arange(0, 64, 2, dtype=np.float32) / 64)
    n = np.arange(4096)
    ang = np.outer(np.maximum(n - 1, 0), inv)
    cos = (np.cos(ang) * scale).astype(np.float32)
    sin = (np.sin(ang) * scale).astype(np.float32)
    cos[0] = scale
    sin[0] = 0.0
    return np.ascontiguousarray(cos), np.ascontiguousarray(sin)


def _pb_tiles(pos_bias):
    out = np.full((ND, NH, 128, 128), -1e30, dtype=np.float32)
    kk = np.arange(128)[:, None]
    qq = np.arange(128)[None, :]
    inv_sorted = OFFSETS
    for dist in range(ND):
        o = 128 * dist + qq - kk
        valid = np.isin(o, inv_sorted)
        idx = np.searchsorted(inv_sorted, np.clip(o, 0, MAXOFF))
        for h in range(NH):
            out[dist, h] = np.where(valid, pos_bias[np.clip(idx, 0, len(inv_sorted) - 1), h], -1e30)
    return out


def kernel(**inputs):
    bq = inputs["L0_attn_bqkv"].astype(np.float32)
    x0 = inputs["L0_attn_x"]
    idb, idf = _ids()

    base = {}
    for nm, cname, O, F, q in PREP_WEIGHTS:
        if nm == "wq_p":
            base[nm] = np.ascontiguousarray(inputs["L1_attn_wq"][_perm_evod()])
        elif nm == "wk_p":
            base[nm] = np.ascontiguousarray(inputs["L1_attn_wk"][_perm_evod()])
        else:
            base[nm] = inputs[nm]
    base["L1_ffn_k0"] = np.ascontiguousarray(inputs["L1_ffn_k0"]).view(np.uint8)
    base["k1_t"] = np.ascontiguousarray(np.ascontiguousarray(inputs["L1_ffn_k1"]).view(np.uint8)
                                        .reshape(32, 128, 32, 128).transpose(2, 0, 1, 3))
    base["k2_t"] = np.ascontiguousarray(np.ascontiguousarray(inputs["L1_ffn_k2"]).view(np.uint8)
                                        .reshape(32, 128, 8, 128).transpose(2, 0, 1, 3))
    base["s_sc"] = np.stack([inputs["L1_ffn_s0"], inputs["L1_ffn_s1"],
                             inputs["L1_ffn_s2"]]).reshape(3, 1).astype(np.float32)
    base["bq_c"] = np.ascontiguousarray(bq[0:D].reshape(8, 128).T)
    base["bg_c"] = np.ascontiguousarray(inputs["L0_attn_bgate"].astype(np.float32).reshape(8, 128).T)
    base["bkv_r"] = np.ascontiguousarray(bq[D:3 * D].reshape(1, 2 * D))
    base["bo_c"] = np.ascontiguousarray(inputs["L0_attn_bout"].astype(np.float32).reshape(8, 128).T)
    base["pb_t"] = _pb_tiles(inputs["L0_attn_pos_bias"].astype(np.float32))
    cq, sq = _rope_tables(0.125)
    ck, sk = _rope_tables(1.0)
    base["cos_q"], base["sin_q"], base["cos_k"], base["sin_k"] = cq, sq, ck, sk
    base["id_bf"] = idb
    base["id_f32"] = idf

    maps = []
    for c in range(N_CORES):
        m = dict(base)
        m["x_fm"] = np.ascontiguousarray(x0[c // 2, (c % 2) * T:(c % 2 + 1) * T, :].T)
        maps.append(m)

    res = run_bass_kernel_spmd(build_fused(), maps, core_ids=list(range(N_CORES))).results

    out = np.empty((4, 2 * T, D), dtype=np.float32)
    for c in range(N_CORES):
        out[c // 2, (c % 2) * T:(c % 2 + 1) * T, :] = res[c]["x_fm_out"].T
    return out


# version 04 (final)
# speedup vs baseline: 1.1783x; 1.1783x over previous
import numpy as np
import ml_dtypes
from contextlib import ExitStack

import concourse.bass as bass
import concourse.mybir as mybir
from concourse.bass_utils import run_bass_kernel_spmd

F32 = mybir.dt.float32
BF16 = mybir.dt.bfloat16
I16 = mybir.dt.int16
U8 = mybir.dt.uint8
AF = mybir.ActivationFunctionType
OP = mybir.AluOpType
AX = mybir.AxisListType
ds = bass.ds

D = 1024
T = 2048
NTG = 4
TG = 512
NFC = D // 128
I_FFN = 2816
NH = 16
HD = 64
OFFSETS = np.array(sorted(set(range(0, 33)) | {48, 64, 96, 128, 192, 256, 384, 512, 768, 1024, 1536}), dtype=np.int32)
MAXOFF = 1536
ND = 13
WIN = MAXOFF + T
NWC = WIN // 128
PAIRS = [[0, 1], [2, 3], [4, 5], [6, 7]]


class Env:

    GLOBAL_SEMS = ("_vch", "_sch", "consts", "cid", "cc")

    def __init__(self, nc):
        self.nc = nc
        self.ctx = ExitStack()
        self._stack = [self.ctx]
        self._sems = {}
        self._alias = {}
        self.prefix = ""

    def push(self, prefix):
        self._stack.append(ExitStack())
        self.prefix = prefix
        self._alias = {}
        self._scope = (prefix.rstrip("_"), self.nc.enter_named_scope(prefix.rstrip("_"), False)[0])

    def pop(self):
        name, sid = self._scope
        self.nc.leave_named_scope(name, sid, False)
        self._stack.pop().close()
        self.prefix = ""
        self._alias = {}

    def sbuf(self, name, shape, dt=F32):
        return self._stack[-1].enter_context(self.nc.sbuf_tensor(self.prefix + name, shape, dt))

    def psum(self, name, shape, dt=F32):
        return self._stack[-1].enter_context(self.nc.psum_tensor(self.prefix + name, shape, dt))

    def sem(self, name):
        if name in self.GLOBAL_SEMS:
            phys = name
        else:
            if name not in self._alias:
                self._alias[name] = f"s{len(self._alias)}"
            phys = self._alias[name]
        if phys not in self._sems:
            h = self.ctx.enter_context(self.nc.semaphore(phys))
            self._sems[phys] = [h, 0]
        return self._sems[phys]

    def inc(self, name, inst, k=1):
        s = self.sem(name)
        inst.then_inc(s[0], k)
        s[1] += k
        return s[1]

    def wait(self, name, eng):
        s = self.sem(name)
        eng.wait_ge(s[0], s[1])

    def wait_val(self, name, eng, val):
        eng.wait_ge(self.sem(name)[0], val)

    def val(self, name):
        return self.sem(name)[1]

    def vsync(self, inst):
        n = self.inc("_vch", inst)
        self.nc.vector.wait_ge(self.sem("_vch")[0], n)
        return inst

    def ssync(self, inst):
        n = self.inc("_sch", inst)
        self.nc.scalar.wait_ge(self.sem("_sch")[0], n)
        return inst

    def barrier(self):
        self.nc.all_engine_barrier()

    def close(self):
        while len(self._stack) > 1:
            self.pop()
        self.ctx.close()


def common_consts(env):
    nc = env.nc
    c = {}
    c["ones_col_f"] = env.sbuf("c_ones_col_f", [128, 1], F32)
    c["ones_col_b"] = env.sbuf("c_ones_col_b", [128, 1], BF16)
    c["ones2_f"] = env.sbuf("c_ones2_f", [2, 128], F32)
    c["ones_row_b"] = env.sbuf("c_ones_row_b", [1, 128], BF16)
    c["neghalf_row_b"] = env.sbuf("c_neghalf_row_b", [1, 128], BF16)
    v = nc.vector
    v.memset(c["ones_col_f"][:, :], 1.0)
    v.memset(c["ones_col_b"][:, :], 1.0)
    v.memset(c["ones2_f"][:, :], 0.0)
    v.memset(c["ones_row_b"][:, :], 1.0)
    v.memset(c["neghalf_row_b"][:, :], -0.5)
    last = v.memset(c["ones2_f"][0:1, :], 1.0)
    for i, val in enumerate([1e-5, -0.5, 0.125]):
        t = env.sbuf(f"c_cap{i}", [1, 1], F32)
        last = v.memset(t[:, :], val)
        nc.const_aps.register(F32, val, t[0:1, 0:1])
    env.inc("consts", last)
    for e in (nc.tensor, nc.scalar, nc.sync, nc.gpsimd):
        env.wait("consts", e)
    return c


def load_idents(env, c, idb, idf):
    nc = env.nc
    c["ident"] = env.sbuf("c_ident", [128, 128], BF16)
    c["identf"] = env.sbuf("c_identf", [128, 128], F32)
    env.inc("cid", nc.gpsimd.dma_start(out=c["ident"][:, :], in_=idb), 16)
    env.inc("cid", nc.gpsimd.dma_start(out=c["identf"][:, :], in_=idf), 16)
    for e in (nc.tensor, nc.vector, nc.scalar, nc.sync, nc.gpsimd):
        env.wait("cid", e)



def fp4_quant_scratch(env, Fmax, tag=""):
    return {
        "amax": env.sbuf(f"q_amax{tag}", [128, Fmax // 16], F32),
        "sc": env.sbuf(f"q_sc{tag}", [128, Fmax // 16], F32),
        "rs": env.sbuf(f"q_rs{tag}", [128, Fmax // 16], F32),
        "xn": env.sbuf(f"q_xn{tag}", [128, Fmax], F32),
        "i16a": env.sbuf(f"q_i16a{tag}", [128, Fmax], I16),
        "i16b": env.sbuf(f"q_i16b{tag}", [128, Fmax], I16),
        "t1": env.sbuf(f"q_t1{tag}", [128, Fmax], F32),
        "A": env.sbuf(f"q_A{tag}", [128, Fmax], F32),
        "B": env.sbuf(f"q_B{tag}", [128, Fmax], F32),
        "q1": env.sbuf(f"q_q1{tag}", [128, Fmax], F32),
    }


def emit_fp4_quant(env, x_in, out_bf, scr):
    v = env.nc.vector
    P, F = x_in.shape
    NB = F // 16
    amax, sc, rs = scr["amax"][:P, :NB], scr["sc"][:P, :NB], scr["rs"][:P, :NB]
    xn = scr["xn"][:P, :F]
    i16a, i16b = scr["i16a"][:P, :F], scr["i16b"][:P, :F]
    t1, A, B, q1 = scr["t1"][:P, :F], scr["A"][:P, :F], scr["B"][:P, :F], scr["q1"][:P, :F]
    x3 = x_in.rearrange("p (b k) -> p b k", k=16)
    env.vsync(v.tensor_reduce(amax, x3, AX.X, OP.max, apply_absolute_value=True))
    env.vsync(v.tensor_scalar(sc, amax, 1.0 / 6.0, 1e-12, OP.mult, OP.max))
    env.vsync(v.reciprocal(rs, sc))
    rs_b = rs.unsqueeze(2).broadcast_to([P, NB, 16])
    sc_b = sc.unsqueeze(2).broadcast_to([P, NB, 16])
    xn3 = xn.rearrange("p (b k) -> p b k", k=16)
    env.vsync(v.tensor_tensor(xn3, x3, rs_b, OP.mult))
    v.tensor_scalar(i16a, xn, 2.0, None, OP.mult)
    env.vsync(v.tensor_scalar(i16b, xn, 1.0, None, OP.mult))
    v.tensor_scalar(t1, i16a, 0.5, 2.0, OP.mult, OP.min)
    v.tensor_scalar(A, i16b, 4.0, 2.0, OP.min, OP.max)
    env.vsync(v.tensor_scalar(B, i16b, -4.0, -2.0, OP.max, OP.min))
    env.vsync(v.scalar_tensor_tensor(q1, t1, -2.0, A, OP.max, OP.add))
    env.vsync(v.tensor_tensor(q1, q1, B, OP.add))
    v.tensor_scalar(A, xn, 5.0, 2.0, OP.is_gt, OP.mult)
    env.vsync(v.tensor_scalar(B, xn, -5.0, -2.0, OP.is_lt, OP.mult))
    env.vsync(v.tensor_tensor(q1, q1, A, OP.add))
    env.vsync(v.tensor_tensor(q1, q1, B, OP.add))
    o3 = out_bf.rearrange("p (b k) -> p b k", k=16)
    q13 = q1.rearrange("p (b k) -> p b k", k=16)
    return v.tensor_tensor(o3, q13, sc_b, OP.mult)



def ln_scratch(env, p_b1, p_b2, tag=""):
    return {
        "xsq": [env.sbuf(f"ln_xsq0{tag}", [128, TG], F32), env.sbuf(f"ln_xsq1{tag}", [128, TG], F32)],
        "p_s1": env.psum(f"ln_ps1{tag}", [1, TG], F32),
        "p_s2": env.psum(f"ln_ps2{tag}", [1, TG], F32),
        "p_b1": p_b1,
        "p_b2": p_b2,
        "m": env.sbuf(f"ln_m{tag}", [2, TG], F32),
        "msq": env.sbuf(f"ln_msq{tag}", [2, TG], F32),
        "var": env.sbuf(f"ln_var{tag}", [2, TG], F32),
        "lnv": env.sbuf(f"ln_lnv{tag}", [2, TG], F32),
        "rstd": env.sbuf(f"ln_rstd{tag}", [2, TG], F32),
        "mr": env.sbuf(f"ln_mr{tag}", [2, TG], F32),
        "tscr": env.sbuf(f"ln_tscr{tag}", [128, TG], F32),
    }


def emit_ln_fm(env, c, x_chunks, tg, xn_chunks, s):
    nc = env.nc
    v, te, se = nc.vector, nc.tensor, nc.scalar
    sl = slice(tg * TG, (tg + 1) * TG)
    mm0 = env.val("ln_mm")
    sq0 = env.val("ln_sq")
    for fc in range(NFC):
        inst = te.matmul(s["p_s1"][:, :], c["ones_col_f"][:, :], x_chunks[fc][:, sl],
                         start=(fc == 0), stop=(fc == NFC - 1))
    env.inc("ln_mm", inst)
    for fc in range(NFC):
        b = fc % 2
        if fc >= 2:
            env.wait_val("ln_mm", se, mm0 + fc)
        env.inc("ln_sq", se.activation(s["xsq"][b][:, :], x_chunks[fc][:, sl], AF.Square))
        env.wait_val("ln_sq", te, sq0 + fc + 1)
        inst = te.matmul(s["p_s2"][:, :], c["ones_col_f"][:, :], s["xsq"][b][:, :],
                         start=(fc == 0), stop=(fc == NFC - 1))
        env.inc("ln_mm", inst)
    env.wait("ln_mm", v)
    env.vsync(v.tensor_scalar(s["m"][0:1, :], s["p_s1"][:, :], 1.0 / D, None, OP.mult))
    env.vsync(v.tensor_tensor(s["msq"][0:1, :], s["m"][0:1, :], s["m"][0:1, :], OP.mult))
    env.inc("ln_dve", v.scalar_tensor_tensor(s["var"][0:1, :], s["p_s2"][:, :], 1.0 / D,
                                             s["msq"][0:1, :], OP.mult, OP.subtract))
    env.wait("ln_dve", se)
    env.ssync(se.activation(s["lnv"][0:1, :], s["var"][0:1, :], AF.Ln, bias=1e-5))
    env.inc("ln_sq", se.activation(s["rstd"][0:1, :], s["lnv"][0:1, :], AF.Exp, scale=-0.5))
    env.wait("ln_sq", v)
    env.inc("ln_dve", v.tensor_tensor(s["mr"][0:1, :], s["m"][0:1, :], s["rstd"][0:1, :], OP.mult))
    env.wait("ln_dve", te)
    te.matmul(s["p_b1"][:, :], c["ones2_f"][:, :], s["rstd"][:, :], start=True, stop=True)
    env.inc("ln_mm", te.matmul(s["p_b2"][:, :], c["ones2_f"][:, :], s["mr"][:, :], start=True, stop=True))
    env.wait("ln_mm", v)
    for fc in range(NFC):
        env.vsync(v.tensor_tensor(s["tscr"][:, :], x_chunks[fc][:, sl], s["p_b1"][:, :], OP.mult))
        env.vsync(v.tensor_tensor(xn_chunks[fc][:, 0:TG], s["tscr"][:, :], s["p_b2"][:, :], OP.subtract))
    env.inc("ln_done", v.engine_nop())



PREP_WEIGHTS = [
    ("L0_ffn_w_gate", "wgq_t", I_FFN, D, True),
    ("L0_ffn_w_up", "wuq_t", I_FFN, D, True),
    ("L0_ffn_w_down", "wdq_t", D, I_FFN, True),
    ("L0_attn_Wqkv", "qkv_t", 3 * D, D, False),
    ("L0_attn_Wgate", "wgate_t", D, D, False),
    ("L0_attn_Wout", "wout_t", D, D, False),
    ("wq_p", "wq_t", D, D, False),
    ("wk_p", "wk_t", D, D, False),
    ("L1_attn_wv", "wv_t", D, D, False),
    ("L1_attn_wo", "wo_t", D, D, False),
]


def emit_prep(env, c, ins, outs):
    nc = env.nc
    wbuf = env.sbuf("p_wbuf", [128, 24 * D], BF16)
    tstrip = env.sbuf("p_tstrip", [128, 3 * D], BF16)
    qscr = fp4_quant_scratch(env, I_FFN)
    qf32 = env.sbuf("p_qf32", [128, I_FFN], F32)

    sy, sc_, gp, v = nc.sync, nc.scalar, nc.gpsimd, nc.vector
    for nm, cname, O, F, quant in PREP_WEIGHTS:
        NOC, NFCw = O // 128, F // 128
        if quant:
            for oc in range(NOC):
                env.wait("pqd", gp)
                env.inc("pld", gp.dma_start(out=qf32[:, :F], in_=ins[nm][oc * 128:(oc + 1) * 128, :]), 16)
                env.wait("pld", v)
                env.inc("pqd", emit_fp4_quant(env, qf32[:, :F], wbuf[:, oc * F:(oc + 1) * F], qscr))
        else:
            for oc in range(NOC):
                env.inc("pld", gp.dma_start(out=wbuf[:, oc * F:(oc + 1) * F], in_=ins[nm][oc * 128:(oc + 1) * 128, :]), 16)
        env.wait("pqd", sy)
        env.wait("pld", sy)
        for fc in range(NFCw):
            for oc in range(NOC):
                env.inc("ptr", sy.dma_start(out=tstrip[:, oc * 128:(oc + 1) * 128],
                                            in_=wbuf[:, oc * F + fc * 128: oc * F + (fc + 1) * 128],
                                            transpose=True), 16)
            env.wait("ptr", sc_)
            env.inc("pst", sc_.dma_start(out=outs[cname][fc, :, :, :].transpose([1, 0, 2]),
                                         in_=tstrip[:, 0:O].rearrange("p (oc k) -> p oc k", k=128)), 16)
            env.wait("pst", sy)
        env.barrier()

    k0 = ins["L1_ffn_k0"]
    top = env.sbuf("p_k0t", [128, 4096], BF16)
    bot = env.sbuf("p_k0b", [128, 4096], BF16)
    dif = env.sbuf("p_k0d", [128, 4096], BF16)
    for fc in range(NFC):
        env.inc("pld", gp.dma_start(out=top[:, :], in_=k0[fc * 128:(fc + 1) * 128, :]), 16)
        env.inc("pld", gp.dma_start(out=bot[:, :], in_=k0[D + fc * 128:D + (fc + 1) * 128, :]), 16)
        env.wait("pld", v)
        env.inc("pqd", v.tensor_tensor(dif[:, :], top[:, :], bot[:, :], OP.subtract))
        env.wait("pqd", sy)
        env.inc("pst2", sy.dma_start(out=outs["beff_t"][:, fc, :, :].transpose([1, 0, 2]),
                                     in_=dif[:, :].rearrange("p (jc k) -> p jc k", k=128)), 16)
        env.barrier()
    env.wait("pst2", sy)
    env.wait("pst", sy)


def build_prep():
    nc = bass.Bass("TRN2", target_bir_lowering=False)
    env = Env(nc)
    ins = {}
    outs = {}
    for nm, cname, O, F, _ in PREP_WEIGHTS:
        ins[nm] = nc.dram_tensor(nm, [O, F], F32, kind="ExternalInput").ap()
        outs[cname] = nc.dram_tensor(cname, [F // 128, O // 128, 128, 128], BF16, kind="ExternalOutput").ap()
    ins["L1_ffn_k0"] = nc.dram_tensor("L1_ffn_k0", [2 * D, 4096], U8, kind="ExternalInput").ap()
    outs["beff_t"] = nc.dram_tensor("beff_t", [32, 8, 128, 128], BF16, kind="ExternalOutput").ap()
    c = common_consts(env)
    emit_prep(env, c, ins, outs)
    env.close()
    return nc



def emit_l0ffn(env, c, io):
    nc = env.nc
    x_io, x_out = io["x_in"], io["x_out"]
    wgq, wuq, wdq = io["wgq"], io["wuq"], io["wdq"]
    ident, identf = c["ident"], c["identf"]

    x_sb = [env.sbuf(f"x{fc}", [128, T], F32) for fc in range(NFC)]
    xn_sb = [env.sbuf(f"xn{fc}", [128, TG], F32) for fc in range(NFC)]
    for fc in range(NFC):
        env.inc("cld", nc.gpsimd.dma_start(out=x_sb[fc][:, :], in_=x_io[fc * 128:(fc + 1) * 128, :]), 16)
    for e in (nc.vector, nc.scalar, nc.tensor, nc.sync):
        env.wait("cld", e)

    PA = env.psum("PA", [128, TG], F32)
    PB = env.psum("PB", [128, TG], F32)
    PD = env.psum("PD", [128, TG], F32)
    PTF = env.psum("PTF", [128, 128], F32)
    PTB = env.psum("PTB", [128, 128], BF16)
    lns = ln_scratch(env, PA, PB)
    qscr = fp4_quant_scratch(env, D)

    xn_tm = env.sbuf("xn_tm", [128, 4 * D], F32)
    xq_tm = env.sbuf("xq_tm", [128, 4 * D], BF16)
    xq_fm = [env.sbuf(f"xqf{fc}", [128, TG], BF16) for fc in range(NFC)]
    w_rhs = env.sbuf("w_rhs", [128, 2 * NFC * TG], BF16)
    hs = env.sbuf("hs", [128, TG], F32)
    h_tm = env.sbuf("h_tm", [128, TG], F32)
    hq_tm = env.sbuf("hq_tm", [128, TG], BF16)
    hq_fm = [env.sbuf(f"hqf{j}", [128, TG], BF16) for j in range(22)]
    wd_sb = env.sbuf("wd_sb", [128, 22 * 128], BF16)

    te, v, se, sy, gp = nc.tensor, nc.vector, nc.scalar, nc.sync, nc.gpsimd

    for tg in range(NTG):
        sl = slice(tg * TG, (tg + 1) * TG)
        emit_ln_fm(env, c, x_sb, tg, xn_sb, lns)
        env.barrier()
        for tc in range(4):
            for fc in range(NFC):
                env.wait("tq", te)
                env.inc("ttr", te.transpose(PTF[:, :], xn_sb[fc][:, tc * 128:(tc + 1) * 128], identf[:, :]))
                env.wait("ttr", v)
                env.inc("tq", v.tensor_copy(xn_tm[:, tc * D + fc * 128: tc * D + (fc + 1) * 128], PTF[:, :]))
        env.barrier()
        for tc in range(4):
            env.inc("qx", emit_fp4_quant(env, xn_tm[:, tc * D:(tc + 1) * D], xq_tm[:, tc * D:(tc + 1) * D], qscr))
        env.wait("qx", te)
        for tc in range(4):
            for fc in range(NFC):
                env.wait("tq2", te)
                env.inc("ttr2", te.transpose(PTB[:, :], xq_tm[:, tc * D + fc * 128: tc * D + (fc + 1) * 128], ident[:, :]))
                env.wait("ttr2", v)
                env.inc("tq2", v.tensor_copy(xq_fm[fc][:, tc * 128:(tc + 1) * 128], PTB[:, :]))
        env.barrier()
        for og in range(6):
            OW = TG if og < 5 else 256
            noc = OW // 128
            for fc in range(NFC):
                env.inc("wld", gp.dma_start(
                    out=w_rhs[:, fc * TG: fc * TG + OW].rearrange("p (oc k) -> p oc k", k=128),
                    in_=wgq[fc, og * 4:og * 4 + noc, :, :].transpose([1, 0, 2])), 16)
                env.inc("wld", gp.dma_start(
                    out=w_rhs[:, (NFC + fc) * TG: (NFC + fc) * TG + OW].rearrange("p (oc k) -> p oc k", k=128),
                    in_=wuq[fc, og * 4:og * 4 + noc, :, :].transpose([1, 0, 2])), 16)
            env.wait("wld", te)
            for tc in range(4):
                for fc in range(NFC):
                    te.matmul(PA[:, :OW], xq_fm[fc][:, tc * 128:(tc + 1) * 128],
                              w_rhs[:, fc * TG: fc * TG + OW], start=(fc == 0), stop=(fc == NFC - 1))
                for fc in range(NFC):
                    inst = te.matmul(PB[:, :OW], xq_fm[fc][:, tc * 128:(tc + 1) * 128],
                                     w_rhs[:, (NFC + fc) * TG: (NFC + fc) * TG + OW],
                                     start=(fc == 0), stop=(fc == NFC - 1))
                env.inc("gumm", inst)
                env.wait("gumm", se)
                env.inc("silu", se.activation(hs[:, :OW], PA[:, :OW], AF.Silu))
                env.wait("silu", v)
                env.vsync(v.tensor_tensor(h_tm[:, :OW], hs[:, :OW], PB[:, :OW], OP.mult))
                env.inc("hq", emit_fp4_quant(env, h_tm[:, :OW], hq_tm[:, :OW], qscr))
                env.wait("hq", te)
                for j in range(noc):
                    env.inc("ttr3", te.transpose(PTB[:, :], hq_tm[:, j * 128:(j + 1) * 128], ident[:, :]))
                    env.wait("ttr3", v)
                    env.inc("tq3", v.tensor_copy(hq_fm[og * 4 + j][:, tc * 128:(tc + 1) * 128], PTB[:, :]))
                    env.wait("tq3", te)
            env.barrier()
        for dc in range(NFC):
            env.wait("dmm", gp)
            env.inc("wld2", gp.dma_start(out=wd_sb[:, :].rearrange("p (j k) -> p j k", k=128),
                                         in_=wdq[:, dc, :, :].transpose([1, 0, 2])), 16)
            env.wait("wld2", te)
            for j in range(22):
                inst = te.matmul(PD[:, :], wd_sb[:, j * 128:(j + 1) * 128], hq_fm[j][:, :],
                                 start=(j == 0), stop=(j == 21))
            env.inc("dmm", inst)
            env.wait("dmm", v)
            env.inc("resid", v.scalar_tensor_tensor(x_sb[dc][:, sl], PD[:, :], 0.5, x_sb[dc][:, sl],
                                                    OP.mult, OP.add))
            env.wait("resid", te)
        env.barrier()

    for fc in range(NFC):
        env.inc("xst", sy.dma_start(out=x_out[fc * 128:(fc + 1) * 128, :], in_=x_sb[fc][:, :]), 16)
    env.wait("xst", sy)


def build_l0ffn():
    nc = bass.Bass("TRN2", target_bir_lowering=False)
    env = Env(nc)
    io = dict(
        x_in=nc.dram_tensor("x_fm", [D, T], F32, kind="ExternalInput").ap(),
        x_out=nc.dram_tensor("x_fm_out", [D, T], F32, kind="ExternalOutput").ap(),
        wgq=nc.dram_tensor("wgq_t", [NFC, 22, 128, 128], BF16, kind="ExternalInput").ap(),
        wuq=nc.dram_tensor("wuq_t", [NFC, 22, 128, 128], BF16, kind="ExternalInput").ap(),
        wdq=nc.dram_tensor("wdq_t", [22, NFC, 128, 128], BF16, kind="ExternalInput").ap(),
    )
    idb = nc.dram_tensor("id_bf", [128, 128], BF16, kind="ExternalInput").ap()
    idf = nc.dram_tensor("id_f32", [128, 128], F32, kind="ExternalInput").ap()
    c = common_consts(env)
    load_idents(env, c, idb, idf)
    emit_l0ffn(env, c, io)
    env.close()
    return nc



def emit_l1ffn(env, c, io):
    nc = env.nc
    x_io, x_out = io["x_in"], io["x_out"]
    beff, k1t, k2t, s_in = io["beff"], io["k1t"], io["k2t"], io["s_in"]

    x_sb = [env.sbuf(f"x{fc}", [128, T], F32) for fc in range(NFC)]
    xn_sb = [env.sbuf(f"xn{fc}", [128, TG], BF16) for fc in range(NFC)]
    for fc in range(NFC):
        env.inc("cld", nc.gpsimd.dma_start(out=x_sb[fc][:, :], in_=x_io[fc * 128:(fc + 1) * 128, :]), 16)

    te, v, se, sy, gp = nc.tensor, nc.vector, nc.scalar, nc.sync, nc.gpsimd

    s_sb = env.sbuf("s_sb", [2, 3], F32)
    env.inc("cld", gp.dma_start(out=s_sb[0:1, :], in_=s_in.transpose([1, 0])), 16)
    for e in (nc.vector, nc.scalar, nc.tensor, nc.sync):
        env.wait("cld", e)
    PA = env.psum("PA", [128, TG], F32)
    PB = env.psum("PB", [128, TG], F32)
    PD = env.psum("PD", [128, TG], F32)
    PS = env.psum("PS", [1, TG], F32)
    mb = env.sbuf("mb", [128, 3], F32)
    env.inc("smm", te.matmul(PA[:, 0:3], c["ones2_f"][:, :], s_sb[:, :], start=True, stop=True))
    env.wait("smm", v)
    v.tensor_scalar(mb[:, 0:1], PA[:, 0:1], 2.0, None, OP.mult)
    v.tensor_scalar(mb[:, 1:2], PA[:, 1:2], 4.0, None, OP.mult)
    env.inc("sdv", v.tensor_scalar(mb[:, 2:3], PA[:, 2:3], 2.0, None, OP.mult))
    env.wait("sdv", te)
    env.barrier()

    lns = ln_scratch(env, PA, PB)
    PD2 = env.psum("PD2", [128, TG], F32)
    PDs = [PD, PD2]
    a1 = [env.sbuf(f"a1_{j}", [128, TG], BF16) for j in range(32)]
    a2 = [env.sbuf(f"a2_{j}", [128, TG], BF16) for j in range(32)]
    w_sb = env.sbuf("w_sb", [128, 2 * 32 * 128], BF16)
    rs_hi = env.sbuf("rs_hi", [1, TG], BF16)
    rs_lo = env.sbuf("rs_lo", [1, TG], BF16)

    def wload(bb, jc, src4, ntiles):
        half = ntiles // 2
        env.inc("wldq", sy.dma_start(
            out=w_sb[:, bb * 4096: bb * 4096 + half * 128].rearrange("p (j k) -> p j k", k=128),
            in_=src4[jc, 0:half, :, :].transpose([1, 0, 2])), 16)
        env.inc("wldq", se.dma_start(
            out=w_sb[:, bb * 4096 + half * 128: bb * 4096 + ntiles * 128].rearrange("p (j k) -> p j k", k=128),
            in_=src4[jc, half:ntiles, :, :].transpose([1, 0, 2])), 16)
        return env.val("wldq")

    for tg in range(NTG):
        sl = slice(tg * TG, (tg + 1) * TG)
        emit_ln_fm(env, c, x_sb, tg, xn_sb, lns)
        env.barrier()
        mm0_ = env.val("l1mm")
        ev0_ = env.val("l1ev")
        ldc = []
        for jc in range(33):
            if jc < 32:
                bb = jc % 2
                if jc >= 2:
                    env.wait_val("l1mm", sy, mm0_ + jc - 1)
                    env.wait_val("l1mm", se, mm0_ + jc - 1)
                ldc.append(wload(bb, jc, beff, NFC))
            if jc >= 1:
                j = jc - 1
                bb = j % 2
                env.wait_val("wldq", te, ldc[j])
                if j >= 2:
                    env.wait_val("l1ev", te, ev0_ + j - 1)
                for fc in range(NFC):
                    inst = te.matmul(PDs[bb][:, :], w_sb[:, bb * 4096 + fc * 128: bb * 4096 + (fc + 1) * 128],
                                     xn_sb[fc][:, 0:TG], start=(fc == 0), stop=(fc == NFC - 1))
                env.inc("l1mm", inst)
                env.wait("l1mm", v)
                env.inc("l1ev", v.tensor_scalar(a1[j][:, :], PDs[bb][:, :], mb[:, 0:1], 0.0, OP.mult, OP.max))
        env.barrier()
        for jc in range(32):
            inst = te.matmul(PS[:, :], c["ones_col_b"][:, :], a1[jc][:, :],
                             start=(jc == 0), stop=(jc == 31))
        env.inc("l1mm", inst)
        env.wait("l1mm", v)
        env.vsync(v.tensor_copy(rs_hi[0:1, :], PS[:, :]))
        env.inc("l1ev", v.tensor_tensor(rs_lo[0:1, :], PS[:, :], rs_hi[0:1, :], OP.subtract))
        env.wait("l1ev", te)
        env.barrier()
        mm1_ = env.val("l2mm")
        ev1_ = env.val("l2ev")
        ldc = []
        for jc in range(33):
            if jc < 32:
                bb = jc % 2
                if jc >= 2:
                    env.wait_val("l2mm", sy, mm1_ + jc - 1)
                    env.wait_val("l2mm", se, mm1_ + jc - 1)
                ldc.append(wload(bb, jc, k1t, 32))
            if jc >= 1:
                j = jc - 1
                bb = j % 2
                env.wait_val("wldq", te, ldc[j])
                if j >= 2:
                    env.wait_val("l2ev", te, ev1_ + j - 1)
                for fc in range(32):
                    te.matmul(PDs[bb][:, :], w_sb[:, bb * 4096 + fc * 128: bb * 4096 + (fc + 1) * 128],
                              a1[fc][:, :], start=(fc == 0), stop=False)
                te.matmul(PDs[bb][:, :], c["neghalf_row_b"][0:1, :], rs_hi[0:1, :], start=False, stop=False)
                inst = te.matmul(PDs[bb][:, :], c["neghalf_row_b"][0:1, :], rs_lo[0:1, :],
                                 start=False, stop=True)
                env.inc("l2mm", inst)
                env.wait("l2mm", v)
                env.inc("l2ev", v.tensor_scalar(a2[j][:, :], PDs[bb][:, :], mb[:, 1:2], 0.0, OP.mult, OP.max))
        env.barrier()
        for jc in range(32):
            inst = te.matmul(PS[:, :], c["ones_col_b"][:, :], a2[jc][:, :],
                             start=(jc == 0), stop=(jc == 31))
        env.inc("l1mm", inst)
        env.wait("l1mm", v)
        env.vsync(v.tensor_copy(rs_hi[0:1, :], PS[:, :]))
        env.inc("l1ev", v.tensor_tensor(rs_lo[0:1, :], PS[:, :], rs_hi[0:1, :], OP.subtract))
        env.wait("l1ev", te)
        env.barrier()
        mm2_ = env.val("l3mm")
        ev2_ = env.val("l3ev")
        ldc = []
        for jc in range(NFC + 1):
            if jc < NFC:
                bb = jc % 2
                if jc >= 2:
                    env.wait_val("l3mm", sy, mm2_ + jc - 1)
                    env.wait_val("l3mm", se, mm2_ + jc - 1)
                ldc.append(wload(bb, jc, k2t, 32))
            if jc >= 1:
                dc = jc - 1
                bb = dc % 2
                env.wait_val("wldq", te, ldc[dc])
                if dc >= 2:
                    env.wait_val("l3ev", te, ev2_ + dc - 1)
                for fc in range(32):
                    te.matmul(PDs[bb][:, :], w_sb[:, bb * 4096 + fc * 128: bb * 4096 + (fc + 1) * 128],
                              a2[fc][:, :], start=(fc == 0), stop=False)
                te.matmul(PDs[bb][:, :], c["neghalf_row_b"][0:1, :], rs_hi[0:1, :], start=False, stop=False)
                inst = te.matmul(PDs[bb][:, :], c["neghalf_row_b"][0:1, :], rs_lo[0:1, :],
                                 start=False, stop=True)
                env.inc("l3mm", inst)
                env.wait("l3mm", v)
                env.vsync(v.tensor_scalar(lns["tscr"][:, :], PDs[bb][:, :], mb[:, 2:3], None, OP.mult))
                env.inc("l3ev", v.tensor_tensor(x_sb[dc][:, sl], x_sb[dc][:, sl], lns["tscr"][:, :], OP.add))
        env.barrier()

    for fc in range(NFC):
        env.inc("xst", sy.dma_start(out=x_out[fc * 128:(fc + 1) * 128, :], in_=x_sb[fc][:, :]), 16)
    env.wait("xst", sy)


def build_l1ffn():
    nc = bass.Bass("TRN2", target_bir_lowering=False)
    env = Env(nc)
    io = dict(
        x_in=nc.dram_tensor("x_fm", [D, T], F32, kind="ExternalInput").ap(),
        x_out=nc.dram_tensor("x_fm_out", [D, T], F32, kind="ExternalOutput").ap(),
        beff=nc.dram_tensor("beff_t", [32, NFC, 128, 128], BF16, kind="ExternalInput").ap(),
        k1t=nc.dram_tensor("k1_t", [32, 32, 128, 128], BF16, kind="ExternalInput").ap(),
        k2t=nc.dram_tensor("k2_t", [NFC, 32, 128, 128], BF16, kind="ExternalInput").ap(),
        s_in=nc.dram_tensor("s_sc", [3, 1], F32, kind="ExternalInput").ap(),
    )
    c = common_consts(env)
    emit_l1ffn(env, c, io)
    env.close()
    return nc



def emit_l1attn_a(env, c, io):
    nc = env.nc
    x_io = io["x_in"]
    wqt, wkt, wvt = io["wqt"], io["wkt"], io["wvt"]
    cos_q, sin_q, cos_k, sin_k = io["cos_q"], io["sin_q"], io["cos_k"], io["sin_k"]
    q_tm, k_tm, v_tm = io["q_tm"], io["k_tm"], io["v_tm"]

    te, v, se, sy, gp = nc.tensor, nc.vector, nc.scalar, nc.sync, nc.gpsimd

    x_sb = [env.sbuf(f"x{fc}", [128, T], F32) for fc in range(NFC)]
    xn_sb = [env.sbuf(f"xn{fc}", [128, TG], BF16) for fc in range(NFC)]
    for fc in range(NFC):
        env.inc("cld", gp.dma_start(out=x_sb[fc][:, :], in_=x_io[fc * 128:(fc + 1) * 128, :]), 16)
    wq_sb = env.sbuf("wq_sb", [128, NFC * D], BF16)
    wk_sb = env.sbuf("wk_sb", [128, NFC * D], BF16)
    wv_sb = env.sbuf("wv_sb", [128, NFC * D], BF16)
    for fc in range(NFC):
        env.inc("cld", gp.dma_start(out=wq_sb[:, fc * D:(fc + 1) * D].rearrange("p (oc k) -> p oc k", k=128),
                                    in_=wqt[fc, :, :, :].transpose([1, 0, 2])), 16)
        env.inc("cld", gp.dma_start(out=wk_sb[:, fc * D:(fc + 1) * D].rearrange("p (oc k) -> p oc k", k=128),
                                    in_=wkt[fc, :, :, :].transpose([1, 0, 2])), 16)
        env.inc("cld", gp.dma_start(out=wv_sb[:, fc * D:(fc + 1) * D].rearrange("p (oc k) -> p oc k", k=128),
                                    in_=wvt[fc, :, :, :].transpose([1, 0, 2])), 16)
    cq_sb = env.sbuf("cq_sb", [128, 2 * 16 * 32], F32)
    sq_sb = env.sbuf("sq_sb", [128, 2 * 16 * 32], F32)
    ck_sb = env.sbuf("ck_sb", [128, 2 * 16 * 32], F32)
    sk_sb = env.sbuf("sk_sb", [128, 2 * 16 * 32], F32)
    for dst, srcv in [(cq_sb, cos_q), (sq_sb, sin_q), (ck_sb, cos_k), (sk_sb, sin_k)]:
        env.inc("cld", gp.dma_start(out=dst[:, :].rearrange("p (par j i) -> p par j i", par=2, j=16),
                                    in_=srcv.rearrange("(par p j) i -> p par j i", par=2, p=128)), 16)
    for e in (nc.vector, nc.scalar, nc.tensor, nc.sync):
        env.wait("cld", e)

    PA = env.psum("PA", [128, TG], F32)
    PB = env.psum("PB", [128, TG], F32)
    PQ = env.psum("PQ", [128, TG], F32)
    lns = ln_scratch(env, PA, PB)
    sA = env.sbuf("sA", [128, 256], F32)
    sB = env.sbuf("sB", [128, 256], F32)
    sC = env.sbuf("sC", [128, 256], F32)
    sD = env.sbuf("sD", [128, 256], F32)
    qstage = env.sbuf("qstage", [128, D], BF16)
    kstage = env.sbuf("kstage", [128, D], BF16)
    vstage = env.sbuf("vstage", [128, NH * 65], BF16)

    def rope_evac(tci, half, Csb, Ssb, stage):
        ev = PQ[:, :].rearrange("p (h s) -> p h s", s=64)[:, :, 0:32]
        od = PQ[:, :].rearrange("p (h s) -> p h s", s=64)[:, :, 32:64]
        par = tci % 2
        Cb = Csb[:, :].rearrange("p (q j i) -> p q j i", q=2, j=16)[:, par, half * 8:(half + 1) * 8, :]
        Sb = Ssb[:, :].rearrange("p (q j i) -> p q j i", q=2, j=16)[:, par, half * 8:(half + 1) * 8, :]
        a3 = sA[:, :].rearrange("p (h s) -> p h s", s=32)
        b3 = sB[:, :].rearrange("p (h s) -> p h s", s=32)
        c3 = sC[:, :].rearrange("p (h s) -> p h s", s=32)
        d3 = sD[:, :].rearrange("p (h s) -> p h s", s=32)
        v.tensor_tensor(a3, ev, Cb, OP.mult)
        v.tensor_tensor(b3, od, Sb, OP.mult)
        v.tensor_tensor(c3, ev, Sb, OP.mult)
        env.vsync(v.tensor_tensor(d3, od, Cb, OP.mult))
        oev = stage[:, half * TG: (half + 1) * TG].rearrange("p (h s) -> p h s", s=64)[:, :, 0:32]
        ood = stage[:, half * TG: (half + 1) * TG].rearrange("p (h s) -> p h s", s=64)[:, :, 32:64]
        v.tensor_tensor(oev, a3, b3, OP.subtract)
        return v.tensor_tensor(ood, c3, d3, OP.add)

    for tg in range(NTG):
        emit_ln_fm(env, c, x_sb, tg, xn_sb, lns)
        env.barrier()
        for tc in range(4):
            tci = tg * 4 + tc
            for half in range(2):
                for fc in range(NFC):
                    inst = te.matmul(PQ[:, :], xn_sb[fc][:, tc * 128:(tc + 1) * 128],
                                     wq_sb[:, fc * D + half * TG: fc * D + (half + 1) * TG],
                                     start=(fc == 0), stop=(fc == NFC - 1))
                env.inc("amm", inst)
                env.wait("amm", v)
                env.inc("aev", rope_evac(tci, half, cq_sb, sq_sb, qstage))
                env.wait("aev", te)
                for fc in range(NFC):
                    inst = te.matmul(PQ[:, :], xn_sb[fc][:, tc * 128:(tc + 1) * 128],
                                     wk_sb[:, fc * D + half * TG: fc * D + (half + 1) * TG],
                                     start=(fc == 0), stop=(fc == NFC - 1))
                env.inc("amm", inst)
                env.wait("amm", v)
                env.inc("aev", rope_evac(tci, half, ck_sb, sk_sb, kstage))
                env.wait("aev", te)
                for fc in range(NFC):
                    inst = te.matmul(PQ[:, :], xn_sb[fc][:, tc * 128:(tc + 1) * 128],
                                     wv_sb[:, fc * D + half * TG: fc * D + (half + 1) * TG],
                                     start=(fc == 0), stop=(fc == NFC - 1))
                env.inc("amm", inst)
                env.wait("amm", v)
                vsl = vstage[:, 65 * 8 * half: 65 * 8 * (half + 1)].rearrange("p (h s) -> p h s", s=65)[:, :, 0:64]
                pv3 = PQ[:, :].rearrange("p (h s) -> p h s", s=64)
                env.inc("aev", v.tensor_copy(vsl, pv3))
                env.wait("aev", te)
            env.wait("aev", v)
            env.inc("aev", v.memset(vstage[:, 64:NH * 65:65], 1.0))
            env.wait("aev", sy)
            env.inc("ast", sy.dma_start(out=q_tm[tci * 128:(tci + 1) * 128, :], in_=qstage[:, :]), 16)
            env.inc("ast", sy.dma_start(out=k_tm[tci * 128:(tci + 1) * 128, :], in_=kstage[:, :]), 16)
            env.inc("ast", sy.dma_start(out=v_tm[tci * 128:(tci + 1) * 128, :], in_=vstage[:, :]), 16)
            env.wait("ast", v)
            env.wait("ast", te)
        env.barrier()
    env.wait("ast", sy)


def build_l1attn_a():
    nc = bass.Bass("TRN2", target_bir_lowering=False)
    env = Env(nc)
    io = dict(
        x_in=nc.dram_tensor("x_fm", [D, T], F32, kind="ExternalInput").ap(),
        wqt=nc.dram_tensor("wq_t", [NFC, NFC, 128, 128], BF16, kind="ExternalInput").ap(),
        wkt=nc.dram_tensor("wk_t", [NFC, NFC, 128, 128], BF16, kind="ExternalInput").ap(),
        wvt=nc.dram_tensor("wv_t", [NFC, NFC, 128, 128], BF16, kind="ExternalInput").ap(),
        cos_q=nc.dram_tensor("cos_q", [4096, 32], F32, kind="ExternalInput").ap(),
        sin_q=nc.dram_tensor("sin_q", [4096, 32], F32, kind="ExternalInput").ap(),
        cos_k=nc.dram_tensor("cos_k", [4096, 32], F32, kind="ExternalInput").ap(),
        sin_k=nc.dram_tensor("sin_k", [4096, 32], F32, kind="ExternalInput").ap(),
        q_tm=nc.dram_tensor("q_tm", [T, D], BF16, kind="ExternalOutput").ap(),
        k_tm=nc.dram_tensor("k_tm", [T, D], BF16, kind="ExternalOutput").ap(),
        v_tm=nc.dram_tensor("v_tm", [T, NH * 65], BF16, kind="ExternalOutput").ap(),
    )
    c = common_consts(env)
    emit_l1attn_a(env, c, io)
    env.close()
    return nc


def emit_l1attn_b(env, c, io, truncb=99):
    def bail():
        sy = env.nc.sync
        for fc in range(NFC):
            env.inc("xst", sy.dma_start(out=io["x_out"][fc * 128:(fc + 1) * 128, :],
                                        in_=io["x_in"][fc * 128:(fc + 1) * 128, :]), 16)
        env.wait("xst", sy)
    nc = env.nc
    POS = T * 16
    x_io, x_out = io["x_in"], io["x_out"]
    q_tm, k_tm, v_tm = io["q_tm"], io["k_tm"], io["v_tm"]
    wot, attn_d = io["wot"], io["attn_d"]
    ident, identf = c["ident"], c["identf"]
    q_pm = q_tm.rearrange("t (j d) -> (t j) d", d=64)
    k_pm = k_tm.rearrange("t (j d) -> (t j) d", d=64)
    v_pm = v_tm.rearrange("t (j e) -> (t j) e", e=65)

    te, v, se, sy, gp = nc.tensor, nc.vector, nc.scalar, nc.sync, nc.gpsimd
    if truncb <= -1:
        bail()
        return
    x_sb = [env.sbuf(f"x{fc}", [128, T], F32) for fc in range(NFC)]
    for fc in range(NFC):
        env.inc("cld", gp.dma_start(out=x_sb[fc][:, :], in_=x_io[fc * 128:(fc + 1) * 128, :]), 16)
    if truncb <= -3:
        env.wait("cld", gp)
        bail()
        return
    wo_sb = env.sbuf("wo_sb", [128, NFC * D], BF16)
    for fc in range(NFC):
        env.inc("cld", gp.dma_start(out=wo_sb[:, fc * D:(fc + 1) * D].rearrange("p (oc k) -> p oc k", k=128),
                                    in_=wot[fc, :, :, :].transpose([1, 0, 2])), 16)
    if truncb <= -2:
        env.wait("cld", gp)
        bail()
        return
    v_sb = env.sbuf("v_sb", [128, 8 * 32 * 65], BF16)
    for hl in range(8):
        env.inc("cld", sy.dma_start(
            out=v_sb[:, hl * 32 * 65:(hl + 1) * 32 * 65].rearrange("p (pc e) -> p pc e", e=65),
            in_=v_pm[hl * 4096:(hl + 1) * 4096, :].rearrange("(pc p) e -> p pc e", p=128)), 16)
    for e in (nc.vector, nc.scalar, nc.tensor, nc.sync):
        env.wait("cld", e)
    if truncb <= 0:
        bail()
        return

    qbuf = env.sbuf("qbuf", [128, D], BF16)
    sqr = env.sbuf("sqr", [128, D], BF16)
    qn = env.sbuf("qn", [128, 16], F32)
    nrm = env.sbuf("nrm", [128, 2], F32)
    nrmb = env.sbuf("nrmb", [128, 2], BF16)
    cstat = env.sbuf("cstat", [1, 8], F32)
    cneg_b = env.sbuf("cneg_b", [1, 2], BF16)
    cb = env.sbuf("cb", [128, 1], F32)
    PTB = env.psum("PTB", [128, 128], BF16)
    PTF = env.psum("PTF", [128, 128], F32)
    for which, src_t in [(0, q_tm), (1, k_tm)]:
        for tc in range(16):
            env.wait("nsq", gp)
            env.inc("nld", gp.dma_start(out=qbuf[:, :], in_=src_t[tc * 128:(tc + 1) * 128, 0:D]), 16)
            env.wait("nld", se)
            env.inc("nsq", se.activation(sqr[:, :], qbuf[:, :], AF.Square, accum_out=qn[:, tc:tc + 1]))
        env.wait("nsq", v)
        env.vsync(v.tensor_reduce(nrm[:, which:which + 1], qn[:, 0:16], AX.X, OP.max))
        env.inc("nstat", v.tensor_copy(nrmb[:, which:which + 1], nrm[:, which:which + 1]))
    env.wait("nstat", te)
    for which in range(2):
        env.inc("ntr", te.transpose(PTB[0:1, :], nrmb[:, which:which + 1], ident[:, :]))
        env.wait("ntr", v)
        env.vsync(v.tensor_reduce(cstat[0:1, which:which + 1], PTB[0:1, :], AX.X, OP.max))
        env.wait("_vch", te)
    env.wait("_vch", se)
    env.ssync(se.activation(cstat[0:1, 2:3], cstat[0:1, 0:1], AF.Sqrt))
    env.inc("nsq", se.activation(cstat[0:1, 3:4], cstat[0:1, 1:2], AF.Sqrt))
    env.wait("nsq", v)
    env.vsync(v.tensor_tensor(cstat[0:1, 4:5], cstat[0:1, 2:3], cstat[0:1, 3:4], OP.mult))
    env.vsync(v.tensor_scalar(cstat[0:1, 5:6], cstat[0:1, 4:5], -0.1275, None, OP.mult))
    env.inc("nstat", v.tensor_copy(cneg_b[0:1, 0:1], cstat[0:1, 5:6]))
    env.wait("nstat", te)
    env.inc("ntr", te.matmul(PTF[:, 0:1], c["ones_row_b"][0:1, :], cneg_b[0:1, 0:1], start=True, stop=True))
    env.wait("ntr", v)
    env.inc("nstat", v.tensor_copy(cb[:, :], PTF[:, 0:1]))
    env.barrier()
    if truncb <= 1:
        bail()
        return

    q_pos = [env.sbuf(f"qp{p}", [128, 4096], BF16) for p in range(4)]
    k_pos = [env.sbuf(f"kp{p}", [128, 4096], BF16) for p in range(4)]
    for src_pm, dst in [(q_pm, q_pos), (k_pm, k_pos)]:
        for hl in range(8):
            pr, rh = hl // 2, (hl % 2) * 64
            for pc in range(32):
                env.wait("bt_ld", te)
                env.wait("bt_tq", gp)
                env.inc("bt_ld", gp.dma_start(out=qbuf[:, 0:64],
                                              in_=src_pm[hl * 4096 + pc * 128: hl * 4096 + (pc + 1) * 128, :]), 16)
                env.wait("bt_ld", te)
                env.inc("bt_tr", te.transpose(PTB[0:64, :], qbuf[:, 0:64], ident[:, :]))
                env.wait("bt_tr", v)
                env.inc("bt_tq", v.tensor_copy(dst[pr][rh:rh + 64, pc * 128:(pc + 1) * 128], PTB[0:64, :]))
    env.barrier()
    if truncb <= 2:
        bail()
        return

    PSA = env.psum("PSA", [128, TG], F32)
    PSB = env.psum("PSB", [128, TG], F32)
    POA = env.psum("POA", [65, TG], F32)
    POB = env.psum("POB", [65, TG], F32)
    aT = env.sbuf("aT", [128, 2 * TG], BF16)
    poT = env.sbuf("poT", [65, TG], F32)
    astage = env.sbuf("astage", [128, 64], BF16)
    for qg in range(8):
        qsl = slice(qg * TG, (qg + 1) * TG)
        for pair in range(4):
            for kc in range(32):
                ksl = slice(kc * 128, (kc + 1) * 128)
                te.matmul(PSA[:, :], k_pos[pair][0:64, ksl], q_pos[pair][0:64, qsl], start=True, stop=True)
                inst = te.matmul(PSB[:, :], k_pos[pair][64:128, ksl], q_pos[pair][64:128, qsl],
                                 start=True, stop=True, tile_position=(64, 0))
                env.inc("smm", inst)
                env.wait("smm", se)
                env.ssync(se.activation(aT[:, 0:TG], PSA[:, :], AF.Exp, bias=cb[:, 0:1]))
                env.inc("sexp", se.activation(aT[:, TG:2 * TG], PSB[:, :], AF.Exp, bias=cb[:, 0:1]))
                env.wait("sexp", te)
                h0 = pair * 2
                te.matmul(POA[:, :], v_sb[:, (h0 * 32 + kc) * 65:(h0 * 32 + kc + 1) * 65], aT[:, 0:TG],
                          start=(kc == 0), stop=(kc == 31))
                inst = te.matmul(POB[:, :], v_sb[:, ((h0 + 1) * 32 + kc) * 65:((h0 + 1) * 32 + kc + 1) * 65],
                                 aT[:, TG:2 * TG], start=(kc == 0), stop=(kc == 31))
                env.inc("avmm", inst)
                env.wait("avmm", se)
            env.wait("avmm", v)
            for hh, PO in ((0, POA), (1, POB)):
                hl = pair * 2 + hh
                env.inc("dvv", v.tensor_copy(poT[:, :], PO[:, :]))
                env.wait("dvv", te)
                for sub in range(4):
                    env.wait("ep_done", te)
                    env.inc("eptr", te.transpose(PTF[:, 0:65], poT[:, sub * 128:(sub + 1) * 128],
                                                 identf[0:65, 0:65]))
                    env.wait("eptr", v)
                    env.vsync(v.reciprocal(qn[:, 2:3], PTF[:, 64:65]))
                    env.vsync(v.tensor_scalar(astage[:, :], PTF[:, 0:64], qn[:, 2:3], None, OP.mult))
                    env.wait("_vch", sy)
                    env.inc("ep_st", sy.dma_start(
                        out=attn_d[hl * 4096 + qg * 512 + sub * 128: hl * 4096 + qg * 512 + (sub + 1) * 128, :],
                        in_=astage[:, :]), 16)
                    env.inc("ep_done", v.engine_nop())
                    env.wait("ep_st", v)
        env.barrier()

    if truncb <= 3:
        bail()
        return
    attn_tv = attn_d.rearrange("(t j) d -> t (j d)", j=16)
    attn_fm = [env.sbuf(f"at{fc}", [128, TG], BF16) for fc in range(NFC)]
    PD = env.psum("PD", [128, TG], F32)
    for tg in range(NTG):
        sl = slice(tg * TG, (tg + 1) * TG)
        env.wait("omm", sy)
        for fc in range(NFC):
            for tc in range(4):
                env.inc("aft", sy.dma_start(out=attn_fm[fc][:, tc * 128:(tc + 1) * 128],
                                            in_=attn_tv[tg * TG + tc * 128: tg * TG + (tc + 1) * 128,
                                                        fc * 128:(fc + 1) * 128],
                                            transpose=True), 16)
        env.wait("aft", te)
        for oc in range(NFC):
            for fc in range(NFC):
                inst = te.matmul(PD[:, :], wo_sb[:, fc * D + oc * 128: fc * D + (oc + 1) * 128],
                                 attn_fm[fc][:, 0:TG], start=(fc == 0), stop=(fc == NFC - 1))
            env.inc("omm", inst)
            env.wait("omm", v)
            env.inc("oev", v.tensor_tensor(x_sb[oc][:, sl], x_sb[oc][:, sl], PD[:, :], OP.add))
            env.wait("oev", te)
    env.barrier()
    for fc in range(NFC):
        env.inc("xst", sy.dma_start(out=x_out[fc * 128:(fc + 1) * 128, :], in_=x_sb[fc][:, :]), 16)
    env.wait("xst", sy)


def build_l1attn_b():
    nc = bass.Bass("TRN2", target_bir_lowering=False)
    env = Env(nc)
    io = dict(
        x_in=nc.dram_tensor("x_fm", [D, T], F32, kind="ExternalInput").ap(),
        x_out=nc.dram_tensor("x_fm_out", [D, T], F32, kind="ExternalOutput").ap(),
        q_tm=nc.dram_tensor("q_tm", [T, D], BF16, kind="ExternalInput").ap(),
        k_tm=nc.dram_tensor("k_tm", [T, D], BF16, kind="ExternalInput").ap(),
        v_tm=nc.dram_tensor("v_tm", [T, NH * 65], BF16, kind="ExternalInput").ap(),
        wot=nc.dram_tensor("wo_t", [NFC, NFC, 128, 128], BF16, kind="ExternalInput").ap(),
        attn_d=nc.dram_tensor("attn_d", [T * 16, 64], BF16, kind="Internal").ap(),
    )
    idb = nc.dram_tensor("id_bf", [128, 128], BF16, kind="ExternalInput").ap()
    idf = nc.dram_tensor("id_f32", [128, 128], F32, kind="ExternalInput").ap()
    c = common_consts(env)
    load_idents(env, c, idb, idf)
    emit_l1attn_b(env, c, io)
    env.close()
    return nc



def emit_l0attn_a(env, c, io):
    nc = env.nc
    x_io = io["x_in"]
    qkvt, wgt = io["qkvt"], io["wgt"]
    bq_c, bg_c, bkv_r = io["bq_c"], io["bg_c"], io["bkv_r"]
    q_fm_d, g_fm_d, k_tm, v_tm = io["q_fm"], io["g_fm"], io["k_tm"], io["v_tm"]

    te, v, se, sy, gp = nc.tensor, nc.vector, nc.scalar, nc.sync, nc.gpsimd

    x_sb = [env.sbuf(f"x{fc}", [128, T], F32) for fc in range(NFC)]
    xn_sb = [env.sbuf(f"xn{fc}", [128, TG], BF16) for fc in range(NFC)]
    for fc in range(NFC):
        env.inc("cld", gp.dma_start(out=x_sb[fc][:, :], in_=x_io[fc * 128:(fc + 1) * 128, :]), 16)
    qkv_sb = env.sbuf("qkv_sb", [128, NFC * 24 * 128], BF16)
    wg_sb = env.sbuf("wg_sb", [128, NFC * D], BF16)
    for fc in range(NFC):
        env.inc("cld", gp.dma_start(out=qkv_sb[:, fc * 3072:(fc + 1) * 3072].rearrange("p (oc k) -> p oc k", k=128),
                                    in_=qkvt[fc, :, :, :].transpose([1, 0, 2])), 16)
        env.inc("cld", gp.dma_start(out=wg_sb[:, fc * D:(fc + 1) * D].rearrange("p (oc k) -> p oc k", k=128),
                                    in_=wgt[fc, :, :, :].transpose([1, 0, 2])), 16)
    bq_sb = env.sbuf("bq_sb", [128, 8], F32)
    bq8 = env.sbuf("bq8", [128, 8], F32)
    bg_sb = env.sbuf("bg_sb", [128, 8], F32)
    bkv_sb = env.sbuf("bkv_sb", [1, 2048], BF16)
    env.inc("cld", gp.dma_start(out=bq_sb[:, :], in_=bq_c), 16)
    env.inc("cld", gp.dma_start(out=bg_sb[:, :], in_=bg_c), 16)
    env.inc("cld", gp.dma_start(out=bkv_sb[:, :], in_=bkv_r), 16)
    for e in (nc.vector, nc.scalar, nc.tensor, nc.sync):
        env.wait("cld", e)
    env.inc("bq8d", v.tensor_scalar(bq8[:, :], bq_sb[:, :], 0.125, None, OP.mult))
    env.wait("bq8d", se)

    PA = env.psum("PA", [128, TG], F32)
    PB = env.psum("PB", [128, TG], F32)
    PQ = env.psum("PQ", [128, TG], F32)
    lns = ln_scratch(env, PA, PB)
    ostage = env.sbuf("ostage", [128, TG], BF16)
    vstage = env.sbuf("vstage", [128, NH * 65], BF16)
    kstage = env.sbuf("kstage", [128, D], BF16)

    for tg in range(NTG):
        sl = slice(tg * TG, (tg + 1) * TG)
        emit_ln_fm(env, c, x_sb, tg, xn_sb, lns)
        env.barrier()
        for kind, wsb, stride, bias, func, outd in (
                ("q", qkv_sb, 3072, bq8, AF.Identity, q_fm_d),
                ("g", wg_sb, 1024, bg_sb, AF.Sigmoid, g_fm_d)):
            for oc in range(NFC):
                for fc in range(NFC):
                    inst = te.matmul(PQ[:, :], wsb[:, fc * stride + oc * 128: fc * stride + (oc + 1) * 128],
                                     xn_sb[fc][:, 0:TG], start=(fc == 0), stop=(fc == NFC - 1))
                env.inc("pmm", inst)
                env.wait("pmm", se)
                if kind == "q":
                    env.inc("pev", se.activation(ostage[:, :], PQ[:, :], func, bias=bias[:, oc:oc + 1], scale=0.125))
                else:
                    env.inc("pev", se.activation(ostage[:, :], PQ[:, :], func, bias=bias[:, oc:oc + 1]))
                env.wait("pev", sy)
                env.inc("pst", sy.dma_start(out=outd[oc * 128:(oc + 1) * 128, sl], in_=ostage[:, :]), 16)
                env.wait("pst", se)
                env.wait("pst", te)
        for tc in range(4):
            tci = tg * 4 + tc
            for base_oc, stage, kind in ((8, kstage, "k"), (16, vstage, "v")):
                for half in range(2):
                    for fc in range(NFC):
                        te.matmul(PQ[:, :], xn_sb[fc][:, tc * 128:(tc + 1) * 128],
                                  qkv_sb[:, (fc * 24 + base_oc + half * 4) * 128: (fc * 24 + base_oc + (half + 1) * 4) * 128],
                                  start=(fc == 0), stop=False)
                    inst = te.matmul(PQ[:, :], c["ones_row_b"][0:1, :],
                                     bkv_sb[0:1, (base_oc - 8) * 128 + half * TG: (base_oc - 8) * 128 + (half + 1) * TG],
                                     start=False, stop=True)
                    env.inc("pmm", inst)
                    env.wait("pmm", v)
                    if kind == "k":
                        env.inc("pev2", v.tensor_copy(stage[:, half * TG:(half + 1) * TG], PQ[:, :]))
                    else:
                        vsl = stage[:, 65 * 8 * half: 65 * 8 * (half + 1)].rearrange("p (h s) -> p h s", s=65)[:, :, 0:64]
                        env.inc("pev2", v.tensor_copy(vsl, PQ[:, :].rearrange("p (h s) -> p h s", s=64)))
                    env.wait("pev2", te)
            env.wait("pev2", v)
            env.inc("pev2", v.memset(vstage[:, 64:NH * 65:65], 1.0))
            env.wait("pev2", sy)
            env.inc("pst", sy.dma_start(out=k_tm[tci * 128:(tci + 1) * 128, :], in_=kstage[:, :]), 16)
            env.inc("pst", sy.dma_start(out=v_tm[tci * 128:(tci + 1) * 128, :], in_=vstage[:, :]), 16)
            env.wait("pst", v)
            env.wait("pst", te)
        env.barrier()
    env.wait("pst", sy)


def build_l0attn_a():
    nc = bass.Bass("TRN2", target_bir_lowering=False)
    env = Env(nc)
    io = dict(
        x_in=nc.dram_tensor("x_fm", [D, T], F32, kind="ExternalInput").ap(),
        qkvt=nc.dram_tensor("qkv_t", [NFC, 24, 128, 128], BF16, kind="ExternalInput").ap(),
        wgt=nc.dram_tensor("wgate_t", [NFC, NFC, 128, 128], BF16, kind="ExternalInput").ap(),
        bq_c=nc.dram_tensor("bq_c", [128, 8], F32, kind="ExternalInput").ap(),
        bg_c=nc.dram_tensor("bg_c", [128, 8], F32, kind="ExternalInput").ap(),
        bkv_r=nc.dram_tensor("bkv_r", [1, 2048], F32, kind="ExternalInput").ap(),
        q_fm=nc.dram_tensor("q_fm", [D, T], BF16, kind="ExternalOutput").ap(),
        g_fm=nc.dram_tensor("g_fm", [D, T], BF16, kind="ExternalOutput").ap(),
        k_tm=nc.dram_tensor("k_tm", [T, D], BF16, kind="ExternalOutput").ap(),
        v_tm=nc.dram_tensor("v_tm", [T, NH * 65], BF16, kind="ExternalOutput").ap(),
    )
    c = common_consts(env)
    emit_l0attn_a(env, c, io)
    env.close()
    return nc



def emit_exchange(env, c, io):
    nc = env.nc
    sy, gp, v = nc.sync, nc.gpsimd, nc.vector
    z = env.sbuf("zrow", [128, NH * 65], BF16)
    env.inc("zpr", v.memset(z[:, :], 0.0))
    env.wait("zpr", sy)
    for i in range(MAXOFF // 128):
        env.inc("zst", sy.dma_start(out=io["kpad"][i * 128:(i + 1) * 128, :], in_=z[:, 0:D]), 16)
        env.inc("zst", sy.dma_start(out=io["vpad"][i * 128:(i + 1) * 128, :], in_=z[:, :]), 16)
    env.inc("cc", gp.collective_compute("AllGather", OP.bypass, ins=[io["k_tm"]],
                                        outs=[io["kgath"]], replica_groups=PAIRS), 1)
    env.inc("cc", gp.collective_compute("AllGather", OP.bypass, ins=[io["v_tm"]],
                                        outs=[io["vgath"]], replica_groups=PAIRS), 1)
    env.wait("cc", sy)
    env.wait("cc", gp)
    CH = T // 4
    for i in range(8):
        eng = sy if i % 2 == 0 else gp
        env.inc("xpd", eng.dma_start(out=io["kpad"][MAXOFF + i * CH: MAXOFF + (i + 1) * CH, :],
                                     in_=io["kgath"][i * CH:(i + 1) * CH, :]), 16)
        env.inc("xpd", eng.dma_start(out=io["vpad"][MAXOFF + i * CH: MAXOFF + (i + 1) * CH, :],
                                     in_=io["vgath"][i * CH:(i + 1) * CH, :]), 16)
    env.wait("zst", sy)
    env.wait("xpd", sy)
    env.wait("zst", gp)
    env.wait("xpd", gp)
    pid_s = sy.partition_id()
    pid_g = gp.partition_id()
    for i in range(WIN // 512):
        env.inc("xwn", sy.dma_start(out=io["k_win"][i * 512:(i + 1) * 512, :],
                                    in_=io["kpad"][ds((pid_s & 1) * T + i * 512, 512), :]), 16)
        env.inc("xwn", gp.dma_start(out=io["v_win"][i * 512:(i + 1) * 512, :],
                                    in_=io["vpad"][ds((pid_g & 1) * T + i * 512, 512), :]), 16)
    env.wait("xwn", sy)
    env.wait("xwn", gp)


def emit_l0attn_b(env, c, io):
    nc = env.nc
    x_io, x_out = io["x_in"], io["x_out"]
    q_fm_d, g_fm_d = io["q_fm"], io["g_fm"]
    k_win, v_win = io["k_win"], io["v_win"]
    wot, bo_c, pb_t, flat_d = io["wout"], io["bo_c"], io["pb_t"], io["flat_d"]
    ident, identf = c["ident"], c["identf"]

    te, v, se, sy, gp = nc.tensor, nc.vector, nc.scalar, nc.sync, nc.gpsimd
    wo_sb = env.sbuf("wo_sb", [128, NFC * D], BF16)
    for fc in range(NFC):
        env.inc("cld", gp.dma_start(out=wo_sb[:, fc * D:(fc + 1) * D].rearrange("p (oc k) -> p oc k", k=128),
                                    in_=wot[fc, :, :, :].transpose([1, 0, 2])), 16)
    q_sb = [env.sbuf(f"qf{fc}", [128, T], BF16) for fc in range(NFC)]
    for fc in range(NFC):
        env.inc("cld", gp.dma_start(out=q_sb[fc][:, :], in_=q_fm_d[fc * 128:(fc + 1) * 128, :]), 16)
    bo_sb = env.sbuf("bo_sb", [128, 8], F32)
    env.inc("cld", gp.dma_start(out=bo_sb[:, :], in_=bo_c), 16)
    for e in (nc.vector, nc.scalar, nc.tensor, nc.sync):
        env.wait("cld", e)

    PTB = env.psum("PTB", [1, 128], BF16)
    PTF = env.psum("PTF", [128, 128], F32)
    pbe_sb = env.sbuf("pbe_sb", [128, ND * NH * 128], BF16)
    pbld = env.sbuf("pbld", [128, 2 * 128], F32)
    pmx = env.sbuf("pmx", [128, ND * NH + 2], F32)
    cstat = env.sbuf("cstat", [1, 8], F32)
    cn_b = env.sbuf("cn_b", [1, 2], BF16)
    cb = env.sbuf("cb", [128, 1], F32)
    mpb = env.sbuf("mpb", [128, 1], F32)
    for i in range(ND * NH):
        d_, h_ = i // NH, i % NH
        b = i % 2
        if i >= 2:
            env.wait_val("pbm", gp, env.val("pbm") - 1)
        env.inc("pbl", gp.dma_start(out=pbld[:, b * 128:(b + 1) * 128], in_=pb_t[d_, h_, :, :]), 16)
        env.wait("pbl", v)
        env.inc("pbm", v.tensor_reduce(pmx[:, i:i + 1], pbld[:, b * 128:(b + 1) * 128], AX.X, OP.max))
    env.wait("pbm", v)
    env.vsync(v.tensor_reduce(pmx[:, ND * NH:ND * NH + 1], pmx[:, 0:ND * NH], AX.X, OP.max))
    pmxb = env.sbuf("pmxb", [128, 1], BF16)
    env.vsync(v.tensor_copy(pmxb[:, :], pmx[:, ND * NH:ND * NH + 1]))
    env.wait("_vch", te)
    env.inc("pbtr", te.transpose(PTB[:, :], pmxb[:, 0:1], ident[:, :]))
    env.wait("pbtr", v)
    env.vsync(v.tensor_reduce(cstat[0:1, 0:1], PTB[0:1, :], AX.X, OP.max))
    env.vsync(v.tensor_copy(cn_b[0:1, 0:1], cstat[0:1, 0:1]))
    env.wait("_vch", te)
    env.inc("pbtr", te.matmul(PTF[:, 0:1], c["ones_row_b"][0:1, :], cn_b[0:1, 0:1], start=True, stop=True))
    env.wait("pbtr", v)
    env.vsync(v.tensor_scalar(mpb[:, :], PTF[:, 0:1], -1.0, None, OP.mult))
    env.wait("_vch", se)
    for i in range(ND * NH):
        d_, h_ = i // NH, i % NH
        b = i % 2
        env.wait_val("pbe", gp, max(0, env.val("pbe") - 1))
        env.inc("pbl", gp.dma_start(out=pbld[:, b * 128:(b + 1) * 128], in_=pb_t[d_, h_, :, :]), 16)
        env.wait("pbl", se)
        env.inc("pbe", se.activation(pbe_sb[:, i * 128:(i + 1) * 128], pbld[:, b * 128:(b + 1) * 128],
                                     AF.Exp, bias=mpb[:, 0:1]))
    env.barrier()

    qbuf = env.sbuf("qbuf", [128, D], BF16)
    sqr = env.sbuf("sqr", [128, D], BF16)
    qn = env.sbuf("qn", [128, NWC], F32)
    nrm = env.sbuf("nrm", [128, 2], F32)
    nrmb = env.sbuf("nrmb", [128, 2], BF16)
    for tc in range(NWC):
        env.wait("nsq", gp)
        env.inc("nld", gp.dma_start(out=qbuf[:, :], in_=k_win[tc * 128:(tc + 1) * 128, 0:D]), 16)
        env.wait("nld", se)
        env.inc("nsq", se.activation(sqr[:, :], qbuf[:, :], AF.Square, accum_out=qn[:, tc:tc + 1]))
    env.wait("nsq", v)
    env.vsync(v.tensor_reduce(nrm[:, 1:2], qn[:, 0:NWC], AX.X, OP.max))
    env.inc("nstat", v.tensor_copy(nrmb[:, 1:2], nrm[:, 1:2]))
    PSA = env.psum("PSA", [128, 128], F32)
    PSB = env.psum("PSB", [128, 128], F32)
    PS1 = env.psum("PS1", [1, TG], F32)
    sqf = env.sbuf("sqf", [128, TG], F32)
    for tgi in range(NTG):
        env.wait("nstat", te)
        for fc in range(NFC):
            if fc >= 1:
                env.wait("nmm", se)
            env.inc("nsq", se.activation(sqf[:, :], q_sb[fc][:, tgi * TG:(tgi + 1) * TG], AF.Square))
            env.wait("nsq", te)
            env.inc("nmm", te.matmul(PS1[:, :], c["ones_col_f"][:, :], sqf[:, :],
                                     start=(fc == 0), stop=(fc == NFC - 1)))
        env.wait("nmm", v)
        env.inc("nstat", v.tensor_reduce(qn[0:1, 20 + tgi:21 + tgi], PS1[:, :], AX.X, OP.max))
    env.wait("nstat", v)
    env.vsync(v.tensor_reduce(nrm[0:1, 0:1], qn[0:1, 20:20 + NTG], AX.X, OP.max))
    env.inc("nstat", v.tensor_copy(nrmb[0:1, 0:1], nrm[0:1, 0:1]))
    env.wait("nstat", te)
    env.inc("pbtr", te.transpose(PTB[:, :], nrmb[:, 1:2], ident[:, :]))
    env.wait("pbtr", v)
    env.vsync(v.tensor_reduce(cstat[0:1, 1:2], PTB[0:1, :], AX.X, OP.max))
    env.vsync(v.tensor_copy(cstat[0:1, 0:1], nrmb[0:1, 0:1]))
    env.wait("_vch", se)
    env.ssync(se.activation(cstat[0:1, 2:3], cstat[0:1, 0:1], AF.Sqrt))
    env.inc("nsq", se.activation(cstat[0:1, 3:4], cstat[0:1, 1:2], AF.Sqrt))
    env.wait("nsq", v)
    env.vsync(v.tensor_tensor(cstat[0:1, 4:5], cstat[0:1, 2:3], cstat[0:1, 3:4], OP.mult))
    env.vsync(v.tensor_scalar(cstat[0:1, 5:6], cstat[0:1, 4:5], -1.02, None, OP.mult))
    env.inc("nstat", v.tensor_copy(cn_b[0:1, 0:1], cstat[0:1, 5:6]))
    env.wait("nstat", te)
    env.inc("pbtr", te.matmul(PTF[:, 0:1], c["ones_row_b"][0:1, :], cn_b[0:1, 0:1], start=True, stop=True))
    env.wait("pbtr", v)
    env.vsync(v.tensor_tensor(cb[:, :], PTF[:, 0:1], mpb[:, :], OP.add))
    env.barrier()

    k_fm = [env.sbuf(f"kf{p}", [128, WIN], BF16) for p in range(NFC)]
    for p in range(NFC):
        for w in range(NWC):
            env.inc("kft", sy.dma_start(out=k_fm[p][:, w * 128:(w + 1) * 128],
                                        in_=k_win[w * 128:(w + 1) * 128, p * 128:(p + 1) * 128],
                                        transpose=True), 16)
    env.wait("kft", te)
    env.barrier()

    POA = env.psum("POA", [65, 128], F32)
    POB = env.psum("POB", [65, 128], F32)
    vpr = [env.sbuf(f"vp{i}", [128, NWC * 130], BF16) for i in range(2)]
    eT = env.sbuf("eT", [128, 2 * 128], BF16)
    aT = env.sbuf("aT", [128, 2 * 128], BF16)
    poT = env.sbuf("poT", [65, 128], F32)
    astage = env.sbuf("astage", [128, 64], BF16)
    vguard = [0, 0]
    kld_counts = [0, 0]
    for pair in range(NFC):
        h0 = 2 * pair
        buf = pair % 2
        env.wait_val("avmm", gp, vguard[buf])
        for wc in range(NWC):
            env.inc("kld", gp.dma_start(out=vpr[buf][:, wc * 130:(wc + 1) * 130],
                                        in_=v_win[wc * 128:(wc + 1) * 128, 65 * h0: 65 * h0 + 130]), 16)
        kld_counts[buf] = env.val("kld")
        env.wait_val("kld", te, kld_counts[buf])
        for qt in range(16):
            qsl = slice(qt * 128, (qt + 1) * 128)
            for d_ in range(ND):
                w = qt + d_
                dist = ND - 1 - d_
                wsl = slice(w * 128, (w + 1) * 128)
                te.matmul(PSA[:, :], k_fm[pair][0:64, wsl], q_sb[pair][0:64, qsl], start=True, stop=True)
                inst = te.matmul(PSB[:, :], k_fm[pair][64:128, wsl], q_sb[pair][64:128, qsl],
                                 start=True, stop=True, tile_position=(64, 0))
                env.inc("smm", inst)
                env.wait("smm", se)
                env.ssync(se.activation(eT[:, 0:128], PSA[:, :], AF.Exp, bias=cb[:, 0:1]))
                env.inc("sexp", se.activation(eT[:, 128:256], PSB[:, :], AF.Exp, bias=cb[:, 0:1]))
                env.wait("sexp", v)
                env.vsync(v.tensor_tensor(aT[:, 0:128], eT[:, 0:128],
                                          pbe_sb[:, (dist * NH + h0) * 128:(dist * NH + h0 + 1) * 128], OP.mult))
                env.inc("aTd", v.tensor_tensor(aT[:, 128:256], eT[:, 128:256],
                                               pbe_sb[:, (dist * NH + h0 + 1) * 128:(dist * NH + h0 + 2) * 128], OP.mult))
                env.wait("aTd", te)
                te.matmul(POA[:, :], vpr[buf][:, w * 130: w * 130 + 65], aT[:, 0:128],
                          start=(d_ == 0), stop=(d_ == ND - 1))
                inst = te.matmul(POB[:, :], vpr[buf][:, w * 130 + 65: w * 130 + 130], aT[:, 128:256],
                                 start=(d_ == 0), stop=(d_ == ND - 1))
                env.inc("avmm", inst)
                env.wait("avmm", v)
            for hh, PO in ((0, POA), (1, POB)):
                hl = h0 + hh
                env.inc("dvv", v.tensor_copy(poT[:, :], PO[:, :]))
                env.wait("dvv", te)
                env.wait("ep_done", te)
                env.inc("eptr", te.transpose(PTF[:, 0:65], poT[:, :], identf[0:65, 0:65]))
                env.wait("eptr", v)
                env.vsync(v.reciprocal(qn[:, 2:3], PTF[:, 64:65]))
                env.vsync(v.tensor_scalar(astage[:, :], PTF[:, 0:64], qn[:, 2:3], None, OP.mult))
                env.wait("_vch", sy)
                env.inc("ep_st", sy.dma_start(out=flat_d[qt * 128:(qt + 1) * 128, hl * 64:(hl + 1) * 64],
                                              in_=astage[:, :]), 16)
                env.inc("ep_done", v.engine_nop())
                env.wait("ep_st", v)
        vguard[buf] = env.val("avmm")
    env.barrier()

    PD = env.psum("PD", [128, TG], F32)
    fg_sb = [env.sbuf(f"fg{fc}", [128, TG], BF16) for fc in range(NFC)]
    gbuf = env.sbuf("gbuf", [128, TG], BF16)
    xres = env.sbuf("xres", [128, TG], F32)
    for tg in range(NTG):
        sl = slice(tg * TG, (tg + 1) * TG)
        env.wait("omm", sy)
        env.wait("omm", gp)
        for fc in range(NFC):
            for tc in range(4):
                env.inc("aft", sy.dma_start(out=fg_sb[fc][:, tc * 128:(tc + 1) * 128],
                                            in_=flat_d[tg * TG + tc * 128: tg * TG + (tc + 1) * 128,
                                                       fc * 128:(fc + 1) * 128],
                                            transpose=True), 16)
        for fc in range(NFC):
            env.inc("gld", gp.dma_start(out=gbuf[:, :], in_=g_fm_d[fc * 128:(fc + 1) * 128, sl]), 16)
            env.wait("gld", v)
            env.wait("aft", v)
            env.inc("fgd", v.tensor_tensor(fg_sb[fc][:, 0:TG], fg_sb[fc][:, 0:TG], gbuf[:, :], OP.mult))
            env.wait("fgd", gp)
        env.wait("fgd", te)
        for oc in range(NFC):
            env.wait("oev", gp)
            env.inc("xld", gp.dma_start(out=xres[:, :], in_=x_io[oc * 128:(oc + 1) * 128, sl]), 16)
            for fc in range(NFC):
                inst = te.matmul(PD[:, :], wo_sb[:, fc * D + oc * 128: fc * D + (oc + 1) * 128],
                                 fg_sb[fc][:, 0:TG], start=(fc == 0), stop=(fc == NFC - 1))
            env.inc("omm", inst)
            env.wait("omm", v)
            env.wait("xld", v)
            env.vsync(v.scalar_tensor_tensor(xres[:, :], PD[:, :], bo_sb[:, oc:oc + 1],
                                             xres[:, :], OP.add, OP.add))
            env.wait("_vch", sy)
            env.inc("oev", sy.dma_start(out=x_out[oc * 128:(oc + 1) * 128, sl], in_=xres[:, :]), 16)
            env.wait("oev", te)
    env.wait("oev", sy)


def build_l0attn_b():
    nc = bass.Bass("TRN2", target_bir_lowering=False)
    env = Env(nc)
    io = dict(
        x_in=nc.dram_tensor("x_fm", [D, T], F32, kind="ExternalInput").ap(),
        x_out=nc.dram_tensor("x_fm_out", [D, T], F32, kind="ExternalOutput").ap(),
        q_fm=nc.dram_tensor("q_fm", [D, T], BF16, kind="ExternalInput").ap(),
        g_fm=nc.dram_tensor("g_fm", [D, T], BF16, kind="ExternalInput").ap(),
        k_win=nc.dram_tensor("k_win", [WIN, D], BF16, kind="ExternalInput").ap(),
        v_win=nc.dram_tensor("v_win", [WIN, NH * 65], BF16, kind="ExternalInput").ap(),
        wout=nc.dram_tensor("wout_t", [NFC, NFC, 128, 128], BF16, kind="ExternalInput").ap(),
        bo_c=nc.dram_tensor("bo_c", [128, 8], F32, kind="ExternalInput").ap(),
        pb_t=nc.dram_tensor("pb_t", [ND, NH, 128, 128], F32, kind="ExternalInput").ap(),
        flat_d=nc.dram_tensor("flat_d", [T, D], BF16, kind="Internal").ap(),
    )
    idb = nc.dram_tensor("id_bf", [128, 128], BF16, kind="ExternalInput").ap()
    idf = nc.dram_tensor("id_f32", [128, 128], F32, kind="ExternalInput").ap()
    c = common_consts(env)
    load_idents(env, c, idb, idf)
    emit_l0attn_b(env, c, io)
    env.close()
    return nc



def build_fused(upto=9):
    nc = bass.Bass("TRN2", target_bir_lowering=False)
    env = Env(nc)
    ins = {}
    caches = {}
    for nm, cname, O, F, _ in PREP_WEIGHTS:
        ins[nm] = nc.dram_tensor(nm, [O, F], F32, kind="ExternalInput").ap()
        caches[cname] = nc.dram_tensor(cname, [F // 128, O // 128, 128, 128], BF16, kind="Internal").ap()
    ins["L1_ffn_k0"] = nc.dram_tensor("L1_ffn_k0", [2 * D, 4096], U8, kind="ExternalInput").ap()
    caches["beff_t"] = nc.dram_tensor("beff_t", [32, 8, 128, 128], BF16, kind="Internal").ap()

    x_in = nc.dram_tensor("x_fm", [D, T], F32, kind="ExternalInput").ap()
    x_out = nc.dram_tensor("x_fm_out", [D, T], F32, kind="ExternalOutput").ap()
    x_res = nc.dram_tensor("x_res", [D, T], F32, kind="Internal").ap()
    k1t = nc.dram_tensor("k1_t", [32, 32, 128, 128], BF16, kind="ExternalInput").ap()
    k2t = nc.dram_tensor("k2_t", [NFC, 32, 128, 128], BF16, kind="ExternalInput").ap()
    s_in = nc.dram_tensor("s_sc", [3, 1], F32, kind="ExternalInput").ap()
    bq_c = nc.dram_tensor("bq_c", [128, 8], F32, kind="ExternalInput").ap()
    bg_c = nc.dram_tensor("bg_c", [128, 8], F32, kind="ExternalInput").ap()
    bkv_r = nc.dram_tensor("bkv_r", [1, 2048], F32, kind="ExternalInput").ap()
    bo_c = nc.dram_tensor("bo_c", [128, 8], F32, kind="ExternalInput").ap()
    pb_t = nc.dram_tensor("pb_t", [ND, NH, 128, 128], F32, kind="ExternalInput").ap()
    cos_q = nc.dram_tensor("cos_q", [4096, 32], F32, kind="ExternalInput").ap()
    sin_q = nc.dram_tensor("sin_q", [4096, 32], F32, kind="ExternalInput").ap()
    cos_k = nc.dram_tensor("cos_k", [4096, 32], F32, kind="ExternalInput").ap()
    sin_k = nc.dram_tensor("sin_k", [4096, 32], F32, kind="ExternalInput").ap()
    idb = nc.dram_tensor("id_bf", [128, 128], BF16, kind="ExternalInput").ap()
    idf = nc.dram_tensor("id_f32", [128, 128], F32, kind="ExternalInput").ap()

    q_fm0 = nc.dram_tensor("q_fm0", [D, T], BF16, kind="Internal").ap()
    g_fm0 = nc.dram_tensor("g_fm0", [D, T], BF16, kind="Internal").ap()
    k_tm0 = nc.dram_tensor("k_tm0", [T, D], BF16, kind="Internal").ap()
    v_tm0 = nc.dram_tensor("v_tm0", [T, NH * 65], BF16, kind="Internal").ap()
    kgath = nc.dram_tensor("kgath", [2 * T, D], BF16, kind="Internal").ap()
    vgath = nc.dram_tensor("vgath", [2 * T, NH * 65], BF16, kind="Internal").ap()
    kpad = nc.dram_tensor("kpad", [MAXOFF + 2 * T, D], BF16, kind="Internal").ap()
    vpad = nc.dram_tensor("vpad", [MAXOFF + 2 * T, NH * 65], BF16, kind="Internal").ap()
    k_win = nc.dram_tensor("k_win", [WIN, D], BF16, kind="Internal").ap()
    v_win = nc.dram_tensor("v_win", [WIN, NH * 65], BF16, kind="Internal").ap()
    flat_d = nc.dram_tensor("flat_d", [T, D], BF16, kind="Internal").ap()
    q_tm1 = nc.dram_tensor("q_tm1", [T, D], BF16, kind="Internal").ap()
    k_tm1 = nc.dram_tensor("k_tm1", [T, D], BF16, kind="Internal").ap()
    v_tm1 = nc.dram_tensor("v_tm1", [T, NH * 65], BF16, kind="Internal").ap()
    attn_d = nc.dram_tensor("attn_d", [T * 16, 64], BF16, kind="Internal").ap()

    c = common_consts(env)
    load_idents(env, c, idb, idf)

    env.push("pp_")
    emit_prep(env, c, ins, caches)
    env.barrier()
    env.pop()

    def ffn0(tag, xi, xo):
        env.push(tag)
        emit_l0ffn(env, c, dict(x_in=xi, x_out=xo, wgq=caches["wgq_t"], wuq=caches["wuq_t"],
                                wdq=caches["wdq_t"]))
        env.barrier()
        env.pop()

    def ffn1(tag, xi, xo):
        env.push(tag)
        emit_l1ffn(env, c, dict(x_in=xi, x_out=xo, beff=caches["beff_t"], k1t=k1t, k2t=k2t, s_in=s_in))
        env.barrier()
        env.pop()

    done = [False]

    def fin():
        if done[0]:
            return
        done[0] = True
        sy = nc.sync
        env.push("fin_")
        for fc in range(NFC):
            env.inc("fxs", sy.dma_start(out=x_out[fc * 128:(fc + 1) * 128, :],
                                        in_=x_res[fc * 128:(fc + 1) * 128, :]), 16)
        env.wait("fxs", sy)
        env.pop()
        env.close()

    if upto == 80:
        env.push("a1_")
        emit_l1attn_a(env, c, dict(x_in=x_in, wqt=caches["wq_t"], wkt=caches["wk_t"],
                                   wvt=caches["wv_t"], cos_q=cos_q, sin_q=sin_q, cos_k=cos_k,
                                   sin_k=sin_k, q_tm=q_tm1, k_tm=k_tm1, v_tm=v_tm1))
        env.barrier()
        env.pop()
        env.push("b1_")
        import os as _os
        emit_l1attn_b(env, c, dict(x_in=x_in, x_out=x_res, q_tm=q_tm1, k_tm=k_tm1, v_tm=v_tm1,
                                   wot=caches["wo_t"], attn_d=attn_d),
                      truncb=int(_os.environ.get("TRUNCB", "99")))
        env.barrier()
        env.pop()
        fin()
        return nc

    ffn0("f0_", x_in, x_res)
    if upto <= 1:
        fin()
        return nc

    env.push("a0_")
    emit_l0attn_a(env, c, dict(x_in=x_res, qkvt=caches["qkv_t"], wgt=caches["wgate_t"],
                               bq_c=bq_c, bg_c=bg_c, bkv_r=bkv_r,
                               q_fm=q_fm0, g_fm=g_fm0, k_tm=k_tm0, v_tm=v_tm0))
    env.barrier()
    env.pop()
    if upto <= 2:
        fin()
        return nc

    env.push("x0_")
    emit_exchange(env, c, dict(k_tm=k_tm0, v_tm=v_tm0, kgath=kgath, vgath=vgath,
                               kpad=kpad, vpad=vpad, k_win=k_win, v_win=v_win))
    env.barrier()
    env.pop()
    if upto <= 3:
        fin()
        return nc

    env.push("b0_")
    emit_l0attn_b(env, c, dict(x_in=x_res, x_out=x_res, q_fm=q_fm0, g_fm=g_fm0,
                               k_win=k_win, v_win=v_win, wout=caches["wout_t"],
                               bo_c=bo_c, pb_t=pb_t, flat_d=flat_d))
    env.barrier()
    env.pop()
    if upto <= 4:
        fin()
        return nc

    ffn0("f1_", x_res, x_res)
    if upto <= 5:
        fin()
        return nc
    ffn1("g0_", x_res, x_res)
    if upto <= 6:
        fin()
        return nc

    env.push("a1_")
    emit_l1attn_a(env, c, dict(x_in=x_res, wqt=caches["wq_t"], wkt=caches["wk_t"], wvt=caches["wv_t"],
                               cos_q=cos_q, sin_q=sin_q, cos_k=cos_k, sin_k=sin_k,
                               q_tm=q_tm1, k_tm=k_tm1, v_tm=v_tm1))
    env.barrier()
    env.pop()
    if upto <= 7:
        fin()
        return nc

    env.push("b1_")
    emit_l1attn_b(env, c, dict(x_in=x_res, x_out=x_res, q_tm=q_tm1, k_tm=k_tm1, v_tm=v_tm1,
                               wot=caches["wo_t"], attn_d=attn_d))
    env.barrier()
    env.pop()
    if upto <= 8:
        fin()
        return nc

    ffn1("g1_", x_res, x_out)
    env.close()
    return nc

N_CORES = 8
_ID_BF = None
_ID_F32 = None


def _ids():
    global _ID_BF, _ID_F32
    if _ID_BF is None:
        _ID_BF = np.eye(128, dtype=ml_dtypes.bfloat16)
        _ID_F32 = np.eye(128, dtype=np.float32)
    return _ID_BF, _ID_F32


def _perm_evod():
    idx = []
    for h in range(16):
        idx += [h * 64 + 2 * i for i in range(32)]
        idx += [h * 64 + 2 * i + 1 for i in range(32)]
    return np.array(idx)


def _rope_tables(scale):
    inv = 1.0 / 10000.0 ** (np.arange(0, 64, 2, dtype=np.float32) / 64)
    n = np.arange(4096)
    ang = np.outer(np.maximum(n - 1, 0), inv)
    cos = (np.cos(ang) * scale).astype(np.float32)
    sin = (np.sin(ang) * scale).astype(np.float32)
    cos[0] = scale
    sin[0] = 0.0
    return np.ascontiguousarray(cos), np.ascontiguousarray(sin)


def _pb_tiles(pos_bias):
    out = np.full((ND, NH, 128, 128), -1e30, dtype=np.float32)
    kk = np.arange(128)[:, None]
    qq = np.arange(128)[None, :]
    inv_sorted = OFFSETS
    for dist in range(ND):
        o = 128 * dist + qq - kk
        valid = np.isin(o, inv_sorted)
        idx = np.searchsorted(inv_sorted, np.clip(o, 0, MAXOFF))
        for h in range(NH):
            out[dist, h] = np.where(valid, pos_bias[np.clip(idx, 0, len(inv_sorted) - 1), h], -1e30)
    return out


def kernel(**inputs):
    bq = inputs["L0_attn_bqkv"].astype(np.float32)
    x0 = inputs["L0_attn_x"]
    idb, idf = _ids()

    base = {}
    for nm, cname, O, F, q in PREP_WEIGHTS:
        if nm == "wq_p":
            base[nm] = np.ascontiguousarray(inputs["L1_attn_wq"][_perm_evod()])
        elif nm == "wk_p":
            base[nm] = np.ascontiguousarray(inputs["L1_attn_wk"][_perm_evod()])
        else:
            base[nm] = inputs[nm]
    base["L1_ffn_k0"] = np.ascontiguousarray(inputs["L1_ffn_k0"]).view(np.uint8)
    base["k1_t"] = np.ascontiguousarray(np.ascontiguousarray(inputs["L1_ffn_k1"]).view(np.uint8)
                                        .reshape(32, 128, 32, 128).transpose(2, 0, 1, 3)
                                        .astype(ml_dtypes.bfloat16))
    base["k2_t"] = np.ascontiguousarray(np.ascontiguousarray(inputs["L1_ffn_k2"]).view(np.uint8)
                                        .reshape(32, 128, 8, 128).transpose(2, 0, 1, 3)
                                        .astype(ml_dtypes.bfloat16))
    base["s_sc"] = np.stack([inputs["L1_ffn_s0"], inputs["L1_ffn_s1"],
                             inputs["L1_ffn_s2"]]).reshape(3, 1).astype(np.float32)
    base["bq_c"] = np.ascontiguousarray(bq[0:D].reshape(8, 128).T)
    base["bg_c"] = np.ascontiguousarray(inputs["L0_attn_bgate"].astype(np.float32).reshape(8, 128).T)
    base["bkv_r"] = np.ascontiguousarray(bq[D:3 * D].reshape(1, 2 * D))
    base["bo_c"] = np.ascontiguousarray(inputs["L0_attn_bout"].astype(np.float32).reshape(8, 128).T)
    base["pb_t"] = _pb_tiles(inputs["L0_attn_pos_bias"].astype(np.float32))
    cq, sq = _rope_tables(0.125)
    ck, sk = _rope_tables(1.0)
    base["cos_q"], base["sin_q"], base["cos_k"], base["sin_k"] = cq, sq, ck, sk
    base["id_bf"] = idb
    base["id_f32"] = idf

    maps = []
    for c in range(N_CORES):
        m = dict(base)
        m["x_fm"] = np.ascontiguousarray(x0[c // 2, (c % 2) * T:(c % 2 + 1) * T, :].T)
        maps.append(m)

    res = run_bass_kernel_spmd(build_fused(), maps, core_ids=list(range(N_CORES))).results

    out = np.empty((4, 2 * T, D), dtype=np.float32)
    for c in range(N_CORES):
        out[c // 2, (c % 2) * T:(c % 2 + 1) * T, :] = res[c]["x_fm_out"].T
    return out
